# version 07
# baseline (speedup 1.0000x reference)
import numpy as np
import concourse.bass as bass
import concourse.mybir as mybir
from concourse.bass_utils import run_bass_kernel_spmd

F32 = mybir.dt.float32
BF16 = mybir.dt.bfloat16
AF = mybir.ActivationFunctionType
OP = mybir.AluOpType
AX = mybir.AxisListType

B = 2
S = 4096
D = 2048
HH = 5632
NCORE = 8
CSCALE = float(1.0 / np.sqrt(128.0))
LAM_INIT = 0.8
RMS_EPS = 1.1920928955078125e-07
LN_EPS = 1e-5
FFN_EPS = 1e-8
NHC = HH // 128
NDC = D // 128
NKT = S // 128
NQC = S // 512
NRC = 4
RCW = 256


class Sched:

    INC = {"tensor": 1, "vector": 1, "scalar": 1, "sync": 16, "gpsimd": 16}
    ENGS = ("sync", "scalar", "vector", "tensor", "gpsimd")

    def __init__(self):
        self.ops = []

    def add(self, eng, fn, reads=(), writes=()):
        self.ops.append(("op", eng, fn, tuple(reads), tuple(writes)))

    def fence(self):
        self.ops.append(("fence",))

    def emit(self, nc):
        DGRP = {"ident": "dconst", "identb": "dconst", "onespb": "dconst",
                "onesc": "dconst", "hsel": "dconst", "onesr": "dconst",
                "lkp": "dconst", "lqd": "dconst",
                "wraw": "dwraw", "xt40": "dxt40", "xt41": "dxt41",
                "wotraw": "dwot",
                "fw1": "dfw1", "fw2": "dfw2", "fw3": "dfw3",
                "xres": "dxres", "outw": "doutw", "yt": "dgw"}
        semnames = (list(self.INC) + sorted(set(DGRP.values()))
                    + sorted(set(v + "g" for v in DGRP.values())))
        sems = {s: nc.alloc_semaphore("sem_" + s) for s in semnames}
        count = {s: 0 for s in semnames}
        idx = {e: 0 for e in self.INC}
        last_drain = {e: -1 for e in self.INC}
        water = {e: {s: 0 for s in semnames} for e in self.INC}
        last_write = {}
        readers = {}
        streams = {e: [] for e in self.INC}
        rv_idx = {}
        psum_read = {"vector": 0, "scalar": 0}

        def wait(eng, sname, val):
            if val > water[eng][sname]:
                streams[eng].append(("wait", sems[sname], val))
                water[eng][sname] = val

        for item in self.ops:
            if item[0] == "fence":
                snap = dict(count)
                for eng in self.INC:
                    for sname in semnames:
                        if sname != eng:
                            wait(eng, sname, snap[sname])
                continue
            _, eng, fn, reads, writes = item
            is_dma = eng in ("sync", "gpsimd")
            if is_dma:
                assert writes, "DMA op needs a write key"
                grp = DGRP[writes[0]] + ("g" if eng == "gpsimd" else "")
            need = {}
            drain = False
            for bb in reads:
                lw = last_write.get(bb)
                if lw is None:
                    continue
                we, wsem, wval, wi = lw
                if we == eng and not is_dma:
                    if we in ("vector", "scalar") and wi > last_drain[eng]:
                        drain = True
                elif we != eng:
                    need[wsem] = max(need.get(wsem, 0), wval)
            for bb in writes:
                lw = last_write.get(bb)
                if lw is not None:
                    we, wsem, wval, wi = lw
                    if we != eng:
                        need[wsem] = max(need.get(wsem, 0), wval)
                    elif not is_dma and we in ("vector", "scalar")                             and wi > last_drain[eng]:
                        drain = True
                for re, (rsem, rval) in readers.get(bb, {}).items():
                    if re != eng:
                        need[rsem] = max(need.get(rsem, 0), rval)
                    elif not is_dma and re in ("vector", "scalar"):
                        rwi = rv_idx.get((bb, re), -1)
                        if rwi > last_drain[eng]:
                            drain = True
            reads_psum = (eng in ("vector", "scalar")
                          and any(bb.startswith("pb") for bb in reads))
            if reads_psum:
                other = "scalar" if eng == "vector" else "vector"
                if psum_read[other]:
                    need[other] = max(need.get(other, 0), psum_read[other])
            if drain:
                streams[eng].append(("drain",))
                last_drain[eng] = idx[eng] - 1
            for sname, val in need.items():
                wait(eng, sname, val)
            sname = grp if is_dma else eng
            streams[eng].append(("op", fn, sems[sname], self.INC[eng]))
            count[sname] += self.INC[eng]
            idx[eng] += 1
            if reads_psum:
                psum_read[eng] = count[sname]
            for bb in writes:
                last_write[bb] = (eng, sname, count[sname], idx[eng] - 1)
                readers[bb] = {}
                for kk in [k for k in rv_idx if k[0] == bb]:
                    del rv_idx[kk]
            for bb in reads:
                readers.setdefault(bb, {})[eng] = (sname, count[sname])
                rv_idx[(bb, eng)] = idx[eng] - 1

        streams["gpsimd"].append(("wait", sems["doutwg"], count["doutwg"]))

        with nc.Block() as block:
            for eng in self.ENGS:
                def make(eng_name):
                    def body(e):
                        for it in streams[eng_name]:
                            if it[0] == "drain":
                                e.drain()
                            elif it[0] == "wait":
                                e.wait_ge(it[1], it[2])
                            else:
                                it[1](e).then_inc(it[2], it[3])
                    return body
                getattr(block, eng)(make(eng))

def build_nc(dbg=False, stop_u=4, do_ffn=True, stop_qc=None, stop_kt=None,
             stop_sc=None, qkv_mode="full"):
    nc = bass.Bass("TRN2", target_bir_lowering=False)

    xT = nc.dram_tensor("xT", [B, D, S], F32, kind="ExternalInput").ap()
    xresg = nc.dram_tensor("xresg", [B, 128, 4, D], F32, kind="ExternalInput").ap()
    wqT = nc.dram_tensor("wqT", [D, 256], F32, kind="ExternalInput").ap()
    wkT = nc.dram_tensor("wkT", [D, 256], F32, kind="ExternalInput").ap()
    wvT = nc.dram_tensor("wvT", [D, 256], F32, kind="ExternalInput").ap()
    woT = nc.dram_tensor("woT", [D, D], F32, kind="ExternalInput").ap()
    w1b = nc.dram_tensor("w1b", [NHC * D, 128], F32, kind="ExternalInput").ap()
    w2b = nc.dram_tensor("w2b", [NHC * D, 128], F32, kind="ExternalInput").ap()
    w3b = nc.dram_tensor("w3b", [NDC * HH, 128], F32, kind="ExternalInput").ap()
    lkp = nc.dram_tensor("lkp", [128, 132], F32, kind="ExternalInput").ap()
    lqd = nc.dram_tensor("lqd", [128, 4], F32, kind="ExternalInput").ap()
    identx = nc.dram_tensor("identx", [128, 128], F32, kind="ExternalInput").ap()
    identbx = nc.dram_tensor("identbx", [128, 128], BF16, kind="ExternalInput").ap()
    onespbx = nc.dram_tensor("onespbx", [128, 128], BF16, kind="ExternalInput").ap()
    onescx = nc.dram_tensor("onescx", [128, 128], F32, kind="ExternalInput").ap()
    halfselx = nc.dram_tensor("halfselx", [128, 128], F32, kind="ExternalInput").ap()
    onesrx = nc.dram_tensor("onesrx", [1, 128], F32, kind="ExternalInput").ap()
    out = nc.dram_tensor("outp", [B, 128, 4, D], F32, kind="ExternalOutput").ap()

    dbg_t = {}
    if dbg:
        for nm, sh, dt_ in [("dbg_qh", [128, 2048], F32), ("dbg_ql", [128, 2048], F32),
                            ("dbg_kh", [128, 2048], F32), ("dbg_kl", [128, 2048], F32),
                            ("dbg_vh", [128, 2048], F32), ("dbg_vl", [128, 2048], F32),
                            ("dbg_nh", [128, 2048], F32), ("dbg_nl", [128, 2048], F32),
                            ("dbg_m12", [128, 4], F32), ("dbg_th", [128, 3], F32),
                            ("dbg_xlnh", [128, 8192], F32), ("dbg_xlnl", [128, 8192], F32),
                            ("dbg_hih", [128, 5632], F32), ("dbg_hil", [128, 5632], F32),
                            ("dbg_yt", [128, 2048], F32), ("dbg_u1", [128, 512], F32),
                            ("dbg_r1", [1, 512], F32)]:
            dbg_t[nm] = nc.dram_tensor(nm, sh, dt_, kind="ExternalOutput").ap()

    NCOL = 49120
    AR = nc.alloc_sbuf_tensor("arena", [128, NCOL], F32)
    ps = nc.alloc_psum_tensor("ps", [128, 8 * 512], F32)

    cur = [0]

    def alloc(w):
        off = cur[0]
        cur[0] += w
        assert cur[0] <= NCOL, f"arena overflow at {cur[0]}"
        return off

    def a_(off, w, p0=0, p1=128):
        return AR[p0:p1, off:off + w]

    def b_(off, nelem, p0=0, p1=128):
        assert nelem % 2 == 0
        return AR[p0:p1, off:off + nelem // 2].bitcast(BF16)

    def pb(i, w=512, p0=0, p1=128):
        return ps[p0:p1, i * 512:i * 512 + w]

    def pbbf(i, nelem, p0=0, p1=128):
        return ps[p0:p1, i * 512:i * 512 + nelem // 2].bitcast(BF16)

    IDENT = a_(alloc(128), 128)
    IDENTB = b_(alloc(64), 128)
    ONESPB = b_(alloc(64), 128)
    ONESC = a_(alloc(128), 128)
    HSEL = a_(alloc(128), 128)
    ONESR = a_(alloc(128), 128, 0, 1)
    lkp_off = alloc(132)
    LKP = a_(lkp_off, 132)
    LKW1 = a_(lkp_off, 128)
    LKW2 = a_(lkp_off + 1, 128)
    LQD = a_(alloc(4), 4)
    EPSR = a_(alloc(1), 1)
    EPSL = a_(alloc(1), 1)
    M12 = a_(alloc(4), 4)
    th_off = alloc(3)
    THV = [a_(th_off, 1), a_(th_off + 1, 1), a_(th_off + 2, 1)]
    GACC = a_(alloc(1), 1)
    GCOL = a_(alloc(1), 1)
    GSM = a_(alloc(2), 2)
    MROW = a_(alloc(4), 4, 0, 1)
    NX = a_(alloc(4), 4, 0, 2)
    NXC = a_(alloc(1), 1, 0, 2)
    LAMS = a_(alloc(1), 1, 0, 1)
    RT = a_(alloc(64), 64)
    RC_ = a_(alloc(64), 64)
    OT = a_(alloc(128), 128)
    OSQ = a_(alloc(128), 128)
    TMP = a_(alloc(128), 128)
    TMP2 = a_(alloc(128), 128)
    SSQ = a_(alloc(1), 1)
    RMSV = a_(alloc(1), 1)
    MEANV = a_(alloc(1), 1)
    VARV = a_(alloc(1), 1)
    M2V = a_(alloc(1), 1)
    RSTD = a_(alloc(1), 1)
    SQS = a_(alloc(256), 256)

    cur[0] = (cur[0] + 31) // 32 * 32
    CB = cur[0]
    cur[0] += 8192
    WRAW = a_(CB, 2048)
    WH = [b_(CB + 2048 + 1024 * (2 * i), 2048) for i in range(3)]
    WL = [b_(CB + 2048 + 1024 * (2 * i + 1), 2048) for i in range(3)]
    F1RAW = a_(CB, 2048)
    F2RAW = a_(CB + 2048, 2048)
    Q1B = [b_(CB + 4096 + 1024 * i, 2048) for i in range(2)]
    Q2B = [b_(CB + 6144 + 1024 * i, 2048) for i in range(2)]
    F3RAW = a_(CB, 2816)
    T3B = b_(CB + 2816, 2816)
    Q3H = [b_(CB + 4224 + 1408 * i, 2816) for i in range(2)]

    QB = cur[0]
    cur[0] += 12288
    QH = b_(QB, 4096)
    QL = b_(QB + 2048, 4096)
    KH = b_(QB + 4096, 4096)
    KL = b_(QB + 6144, 4096)
    VH = b_(QB + 8192, 4096)
    VL = b_(QB + 10240, 4096)
    WOH = [b_(QB + 2048 * i, 4096) for i in range(2)]
    WOL = [b_(QB + 4096 + 2048 * i, 4096) for i in range(2)]
    HIH = b_(QB, 11264)
    HIL = b_(QB + 5632, 11264)
    T2B = b_(QB + 11264, 2048)

    XLB_ = cur[0]
    cur[0] += 16384
    XLNH = b_(XLB_, 16384)
    XLNL = b_(XLB_ + 8192, 16384)

    SCRB = cur[0]
    cur[0] += 8192
    XT4 = [a_(SCRB + 512 * i, 512) for i in range(2)]
    XPH4 = [b_(SCRB + 1024 + 256 * i, 512) for i in range(2)]
    XPL4 = [b_(SCRB + 1536 + 256 * i, 512) for i in range(2)]
    SQS4 = a_(SCRB + 3072, 512)
    E1R = a_(SCRB, 512)
    E2R = a_(SCRB + 512, 512)
    EH1 = [b_(SCRB + 1024 + 1024 * i, 512) for i in range(2)]
    EL1 = [b_(SCRB + 1280 + 1024 * i, 512) for i in range(2)]
    EH2 = [b_(SCRB + 1536 + 1024 * i, 512) for i in range(2)]
    EL2 = [b_(SCRB + 1792 + 1024 * i, 512) for i in range(2)]
    U1E = a_(SCRB + 3072, 512)
    U2E = a_(SCRB + 3584, 512)
    NH = b_(SCRB + 4096, 4096)
    NL = b_(SCRB + 6144, 4096)
    WOTRAW = a_(SCRB, 4096)
    FT = [a_(SCRB, 2048), a_(SCRB + 2048, 2048)]
    HFT = a_(SCRB + 4096, 256)
    FLNSQ = a_(SCRB + 4352, 2048)
    WOLNSQ = a_(QB + 8192, 2048)
    YF = a_(SCRB + 6400, 256)
    T2B2 = b_(SCRB + 6656, 2048)

    YT = a_(alloc(2048), 2048)
    SG = a_(alloc(256), 256)
    XRES = SG
    GA3 = a_(alloc(3), 3)

    sch = Sched()
    add = sch.add

    def dma(queue, dst, src, reads=(), writes=()):
        add(queue, (lambda d, s: (lambda e: e.dma_start(out=d, in_=s)))(dst, src),
            reads=reads, writes=writes)

    def mm(dst, lhsT, rhs, st, sp, reads, writes, tr=False):
        add("tensor",
            (lambda d, l, r, s1, s2, t: (lambda e: e.matmul(
                d, l, r, start=s1, stop=s2) if not t else e.matmul(
                d, l, r, is_transpose=True)))(dst, lhsT, rhs, st, sp, tr),
            reads=reads, writes=writes)

    def vop(fn, reads, writes):
        add("vector", fn, reads=reads, writes=writes)

    def sop(fn, reads, writes):
        add("scalar", fn, reads=reads, writes=writes)

    def split(hi, lo, src, rkeys, hikey, lokey, scale=None):
        sop((lambda h, s, sc_: (lambda e: e.activation(
            h, s, AF.Copy, scale=sc_)))(hi, src, 1.0 if scale is None else scale),
            tuple(rkeys) + (hikey,), (hikey,))
        vop((lambda l, s, h, sc_: (lambda e: e.scalar_tensor_tensor(
            l, s, sc_, h, op0=OP.mult, op1=OP.subtract)))(
            lo, src, hi, 1.0 if scale is None else scale),
            tuple(rkeys) + (hikey, lokey), (lokey,))

    def dump(nm, src_ap, keys):
        if dbg:
            add("gpsimd", (lambda d, s: (lambda e: e.dma_start(out=d, in_=s)))(
                dbg_t[nm], src_ap), reads=keys, writes=("outw",))

    import os as _os
    _p0parts = 31 if _os.environ.get("SKIPP0", "0") != "1" else 0
    _p0parts = int(_os.environ.get("P0PARTS", str(_p0parts)))
    if _p0parts & 1:
        dma("sync", IDENT, identx, writes=("ident",))
        dma("sync", IDENTB, identbx, writes=("identb",))
        dma("sync", ONESPB, onespbx, writes=("onespb",))
        dma("sync", ONESC, onescx, writes=("onesc",))
        dma("sync", HSEL, halfselx, writes=("hsel",))
        dma("sync", ONESR, onesrx, writes=("onesr",))
        dma("sync", LKP, lkp, writes=("lkp",))
        dma("sync", LQD, lqd, writes=("lqd",))
    if _p0parts & 2:
        vop(lambda e: e.memset(EPSR, RMS_EPS), (), ("epsr",))
        vop(lambda e: e.memset(EPSL, LN_EPS), (), ("epsl",))
    if _p0parts & 4:
        mm(pb(6, 2), LKW1, LQD[:, 0:2], True, True, ("lkp", "lqd"), ("pb6",))
        mm(pb(7, 2), LKW2, LQD[:, 2:4], True, True, ("lkp", "lqd"), ("pb7",))
    if _p0parts & 8:
        sop(lambda e: e.activation(GSM[0:1, 0:1], pb(6, 1, 0, 1), AF.Exp), ("pb6",), ("gsm",))
        sop(lambda e: e.activation(GSM[0:1, 1:2], pb(7, 1, 0, 1), AF.Exp),
            ("pb6", "pb7", "gsm"), ("gsm",))
    if _p0parts & 16:
        vop(lambda e: e.tensor_tensor(LAMS, GSM[0:1, 0:1], GSM[0:1, 1:2], op=OP.subtract),
            ("gsm",), ("lams",))
        vop(lambda e: e.tensor_scalar(LAMS, LAMS, LAM_INIT, None, op0=OP.add),
            ("lams",), ("lams",))

    vop(lambda e: e.memset(GA3, 0.0), (), ("ga3",))
    GG = []
    for _wi, _wsrc in ((0, w1b), (1, w2b), (2, w3b)):
        _nt = (NHC * D if _wi < 2 else NDC * HH) // 128
        for _g0 in range(0, _nt, 16):
            GG.append((_wi, _wsrc, _g0))
    gidx = [0]

    def gamma_step(n):
        for _ in range(n):
            if gidx[0] >= len(GG):
                return
            wi, wsrc, g0 = GG[gidx[0]]
            gidx[0] += 1
            src = bass.AP(wsrc.tensor, g0 * 128 * 128,
                          [[128, 128], [128 * 128, 16], [1, 128]])
            dma("sync", YT, src, writes=("yt",))
            vop(lambda e: e.tensor_reduce(GCOL, YT, axis=AX.X, op=OP.add,
                                          apply_absolute_value=True),
                ("yt", "gcol"), ("gcol",))
            vop((lambda w_: (lambda e: e.tensor_tensor(
                GA3[:, w_:w_ + 1], GA3[:, w_:w_ + 1], GCOL, op=OP.add)))(wi),
                ("gcol", "ga3"), ("ga3",))

    for u in range(stop_u):
        b = u >> 1
        hl = u & 1
        sch.fence()

        for prj, wsrc in enumerate([wqT, wkT, wvT]):
            src = bass.AP(wsrc.tensor, hl * 128,
                          [[256, 128], [128 * 256, NDC], [1, 128]])
            dma("sync", WRAW, src, reads=(f"wh{prj}", f"wl{prj}"),
                writes=("wraw",))
            split(WH[prj], WL[prj], WRAW, ("wraw",), f"wh{prj}", f"wl{prj}")
        vop(lambda e: e.memset(NX[0:2, 0:2], 0.0), ("nx",), ("nx",))

        def qkv_step_load(sc4, dc):
            par = dc % 2
            src = bass.AP(xT.tensor, b * D * S + dc * 128 * S + sc4 * 512,
                          [[S, 128], [1, 512]])
            dma("sync", XT4[par], src, writes=(f"xt4{par}",))
            split(XPH4[par], XPL4[par], XT4[par], (f"xt4{par}",),
                  f"xph4{par}", f"xpl4{par}")

        def qkv_step_mms(sc4, dc):
            par = dc % 2
            for prj in range(3):
                whs = WH[prj][:, dc * 128:(dc + 1) * 128]
                wls = WL[prj][:, dc * 128:(dc + 1) * 128]
                mm(pb(prj, 512), whs, XPH4[par], dc == 0, False,
                   (f"wh{prj}", f"xph4{par}"), (f"pb{prj}",))
                mm(pb(prj, 512), whs, XPL4[par], False, False,
                   (f"wh{prj}", f"xpl4{par}"), (f"pb{prj}",))
                mm(pb(prj, 512), wls, XPH4[par], False, dc == NDC - 1,
                   (f"wl{prj}", f"xph4{par}"), (f"pb{prj}",))

        def qkv_tail(sc4):
            qsl = slice(sc4 * 512, (sc4 + 1) * 512)
            sop((lambda d_: (lambda e: e.activation(d_, pb(0, 512), AF.Copy)))(
                QH[:, qsl]), ("pb0", "qh"), ("qh",))
            sop((lambda d_: (lambda e: e.activation(
                d_, pb(1, 512), AF.Copy, scale=CSCALE)))(KH[:, qsl]),
                ("pb1", "kh"), ("kh",))
            sop((lambda d_: (lambda e: e.activation(d_, pb(2, 512), AF.Copy)))(
                VH[:, qsl]), ("pb2", "vh"), ("vh",))
            vop((lambda l_, h_: (lambda e: e.scalar_tensor_tensor(
                l_, pb(0, 512), 1.0, h_, op0=OP.mult, op1=OP.subtract)))(
                QL[:, qsl], QH[:, qsl]), ("pb0", "qh", "ql"), ("ql",))
            vop((lambda l_, h_: (lambda e: e.scalar_tensor_tensor(
                l_, pb(1, 512), CSCALE, h_, op0=OP.mult, op1=OP.subtract)))(
                KL[:, qsl], KH[:, qsl]), ("pb1", "kh", "kl"), ("kl",))
            vop((lambda l_, h_: (lambda e: e.scalar_tensor_tensor(
                l_, pb(2, 512), 1.0, h_, op0=OP.mult, op1=OP.subtract)))(
                VL[:, qsl], VH[:, qsl]), ("pb2", "vh", "vl"), ("vl",))
            for col, pk in [(0, "pb0"), (1, "pb1")]:
                sc_ = 1.0 if col == 0 else CSCALE
                add("scalar", (lambda p_, s_: (lambda e: e.activation(
                    SQS4, p_, AF.Square, scale=s_)))(pb(col, 512), sc_),
                    reads=(pk, "sqs4"), writes=("sqs4",))
                mm(pb(3, 512), HSEL, SQS4, True, True, ("hsel", "sqs4"), ("pb3",))
                vop(lambda e: e.tensor_reduce(NXC, pb(3, 512, 0, 2), axis=AX.X,
                                              op=OP.max), ("pb3", "nxc"), ("nxc",))
                vop((lambda cl: (lambda e: e.tensor_tensor(
                    NX[0:2, cl:cl + 1], NX[0:2, cl:cl + 1], NXC, op=OP.max)))(col),
                    ("nxc", "nx"), ("nx",))

        nsc4 = stop_sc if stop_sc is not None else S // 512
        steps = [(s4, d) for s4 in range(nsc4) for d in range(NDC)]
        for i, (sc4, dc) in enumerate(steps):
            qkv_step_load(sc4, dc)
            if i > 0:
                psc4, pdc = steps[i - 1]
                qkv_step_mms(psc4, pdc)
                if pdc == NDC - 1:
                    qkv_tail(psc4)
        if steps:
            qkv_step_mms(*steps[-1])
            qkv_tail(steps[-1][0])
        if stop_sc is not None:
            continue

        vop(lambda e: e.tensor_tensor(NX[0:2, 2:3], NX[0:2, 0:1], NX[0:2, 1:2],
                                      op=OP.mult), ("nx",), ("nx",))
        sop(lambda e: e.activation(NX[0:2, 2:3], NX[0:2, 2:3], AF.Ln,
                                   bias=EPSR[0:2, 0:1]), ("nx", "epsr"), ("nx",))
        sop(lambda e: e.activation(NX[0:2, 3:4], NX[0:2, 2:3], AF.Exp, scale=0.5),
            ("nx",), ("nx",))
        sop(lambda e: e.activation(NX[0:2, 3:4], NX[0:2, 3:4], AF.Copy, scale=-1.0),
            ("nx",), ("nx",))
        mm(ps[0:1, 3 * 512:3 * 512 + 2], NX[0:2, 3:4], IDENT[0:2, 0:2],
           True, True, ("nx", "ident"), ("pb3",), tr=True)
        vop(lambda e: e.tensor_copy(MROW[0:1, 0:2], ps[0:1, 3 * 512:3 * 512 + 2]),
            ("pb3", "mrow"), ("mrow",))
        vop(lambda e: e.tensor_copy(MROW[0:1, 2:3], LAMS), ("lams", "mrow"), ("mrow",))
        mm(pb(3, 3), ONESR, MROW[0:1, 0:3], True, True, ("onesr", "mrow"), ("pb3",))
        vop(lambda e: e.tensor_copy(M12[:, 0:3], pb(3, 3)), ("pb3", "m12"), ("m12",))

        if u == 0:
            dump("dbg_qh", QH.bitcast(F32), ("qh",))
            dump("dbg_ql", QL.bitcast(F32), ("ql",))
            dump("dbg_kh", KH.bitcast(F32), ("kh",))
            dump("dbg_kl", KL.bitcast(F32), ("kl",))
            dump("dbg_m12", M12, ("m12",))

        for t in range(NKT):
            tsl = slice(t * 128, (t + 1) * 128)
            mm(pbbf(3, 128 * 128 // 64)[:, 0:128] if False else
               ps[:, 3 * 512:3 * 512 + 64].bitcast(BF16), VH[:, tsl], IDENTB,
               True, True, ("vh", "identb"), ("pb3",), tr=True)
            vop((lambda d: (lambda e: e.tensor_copy(
                d, ps[:, 3 * 512:3 * 512 + 64].bitcast(BF16))))(VH[:, tsl]),
                ("pb3", "vh"), ("vh",))
            mm(ps[:, 3 * 512:3 * 512 + 64].bitcast(BF16), VL[:, tsl], IDENTB,
               True, True, ("vl", "identb"), ("pb3",), tr=True)
            vop((lambda d: (lambda e: e.tensor_copy(
                d, ps[:, 3 * 512:3 * 512 + 64].bitcast(BF16))))(VL[:, tsl]),
                ("pb3", "vl"), ("vl",))

        if u == 0:
            dump("dbg_vh", VH.bitcast(F32), ("vh",))
            dump("dbg_vl", VL.bitcast(F32), ("vl",))

        if stop_qc == 0:
            continue
        sch.fence()

        def scores_exp(qc, kt):
            bs = kt % 2
            S1 = pb(0) if bs == 0 else pb(1)
            S2 = pb(2) if bs == 0 else pb(3)
            k1 = "pb0" if bs == 0 else "pb1"
            k2 = "pb2" if bs == 0 else "pb3"
            qsl = slice(qc * 512, (qc + 1) * 512)
            ksl = slice(kt * 128, (kt + 1) * 128)
            mm(S1, KH[0:64, ksl], QH[0:64, qsl], True, False, ("kh", "qh"), (k1,))
            mm(S1, KH[0:64, ksl], QL[0:64, qsl], False, False, ("kh", "ql"), (k1,))
            mm(S1, KL[0:64, ksl], QH[0:64, qsl], False, True, ("kl", "qh"), (k1,))
            mm(S2, KH[64:128, ksl], QH[64:128, qsl], True, False, ("kh", "qh"), (k2,))
            mm(S2, KH[64:128, ksl], QL[64:128, qsl], False, False, ("kh", "ql"), (k2,))
            mm(S2, KL[64:128, ksl], QH[64:128, qsl], False, True, ("kl", "qh"), (k2,))
            sop((lambda S_: (lambda e: e.activation(E1R, S_, AF.Exp,
                                                    bias=M12[:, 0:1])))(S1),
                (k1, "m12", "e1r"), ("e1r",))
            sop((lambda S_: (lambda e: e.activation(E2R, S_, AF.Exp,
                                                    bias=M12[:, 1:2])))(S2),
                (k2, "m12", "e2r"), ("e2r",))
            vop((lambda d_: (lambda e: e.tensor_copy(d_, E1R)))(EH1[bs]),
                ("e1r", f"eh1{bs}"), (f"eh1{bs}",))
            vop((lambda d_: (lambda e: e.tensor_copy(d_, E2R)))(EH2[bs]),
                ("e2r", f"eh2{bs}"), (f"eh2{bs}",))
            vop((lambda l_, h_: (lambda e: e.scalar_tensor_tensor(
                l_, E1R, 1.0, h_, op0=OP.mult, op1=OP.subtract)))(
                EL1[bs], EH1[bs]), ("e1r", f"eh1{bs}", f"el1{bs}"),
                (f"el1{bs}",))
            vop((lambda l_, h_: (lambda e: e.scalar_tensor_tensor(
                l_, E2R, 1.0, h_, op0=OP.mult, op1=OP.subtract)))(
                EL2[bs], EH2[bs]), ("e2r", f"eh2{bs}", f"el2{bs}"),
                (f"el2{bs}",))

        def usums(kt):
            bs = kt % 2
            st = kt == 0
            sp = kt == NKT - 1
            vsl = slice(kt * 128, (kt + 1) * 128)
            mm(pb(4), VH[:, vsl], EH1[bs], st, False, ("vh", f"eh1{bs}"), ("pb4",))
            mm(pb(4), VH[:, vsl], EL1[bs], False, False, ("vh", f"el1{bs}"), ("pb4",))
            mm(pb(4), VL[:, vsl], EH1[bs], False, sp, ("vl", f"eh1{bs}"), ("pb4",))
            mm(pb(5), VH[:, vsl], EH2[bs], st, False, ("vh", f"eh2{bs}"), ("pb5",))
            mm(pb(5), VH[:, vsl], EL2[bs], False, False, ("vh", f"el2{bs}"), ("pb5",))
            mm(pb(5), VL[:, vsl], EH2[bs], False, sp, ("vl", f"eh2{bs}"), ("pb5",))
            mm(pb(6), ONESPB, EH1[bs], st, False, ("onespb", f"eh1{bs}"), ("pb6",))
            mm(pb(6), ONESPB, EL1[bs], False, sp, ("onespb", f"el1{bs}"), ("pb6",))
            mm(pb(7), ONESPB, EH2[bs], st, False, ("onespb", f"eh2{bs}"), ("pb7",))
            mm(pb(7), ONESPB, EL2[bs], False, sp, ("onespb", f"el2{bs}"), ("pb7",))

        for qc in range(stop_qc if stop_qc is not None else NQC):
            if stop_kt is not None:
                for kt in range(stop_kt):
                    scores_exp(qc, kt)
                break
            for kt in range(NKT):
                scores_exp(qc, kt)
                if kt > 0:
                    usums(kt - 1)
            usums(NKT - 1)
            vop(lambda e: e.tensor_copy(U1E, pb(4)), ("pb4", "u1e"), ("u1e",))
            vop(lambda e: e.tensor_copy(U2E, pb(5)), ("pb5", "u2e"), ("u2e",))
            vop(lambda e: e.tensor_copy(E1R[0:1, :], pb(6, 512, 0, 1)),
                ("pb6", "e1r"), ("r1e", "e1r"))
            vop(lambda e: e.tensor_copy(E2R[0:1, :], pb(7, 512, 0, 1)),
                ("pb7", "e2r"), ("r2e", "e2r"))
            if u == 0 and qc == 0:
                dump("dbg_u1", U1E, ("u1e",))
                dump("dbg_r1", E1R[0:1, :], ("r1e",))
            for t4 in range(4):
                t = qc * 4 + t4
                t4s = slice(t4 * 128, (t4 + 1) * 128)
                mm(pb(0, 128), U1E[:, t4s], IDENT, True, True,
                   ("u1e", "ident"), ("pb0",), tr=True)
                mm(pb(1, 128), U2E[:, t4s], IDENT, True, True,
                   ("u2e", "ident"), ("pb1",), tr=True)
                mm(pb(2, 1), E1R[0:1, t4s], IDENT[0:1, 0:1], True, True,
                   ("r1e", "e1r", "ident"), ("pb2",), tr=True)
                mm(pb(3, 1), E2R[0:1, t4s], IDENT[0:1, 0:1], True, True,
                   ("r2e", "e2r", "ident"), ("pb3",), tr=True)
                vop((lambda tc: (lambda e: e.tensor_copy(RT[:, tc:tc + 1],
                                                         pb(2, 1))))(t),
                    ("pb2", "rt"), ("rt",))
                vop((lambda tc: (lambda e: e.tensor_copy(RT[:, 32 + tc:33 + tc],
                                                         pb(3, 1))))(t),
                    ("pb3", "rt"), ("rt",))
                vop((lambda tc: (lambda e: e.reciprocal(RC_[:, tc:tc + 1],
                                                        RT[:, tc:tc + 1])))(t),
                    ("rt", "rc"), ("rc",))
                vop((lambda tc: (lambda e: e.reciprocal(RC_[:, 32 + tc:33 + tc],
                                                        RT[:, 32 + tc:33 + tc])))(t),
                    ("rt", "rc"), ("rc",))
                vop((lambda tc: (lambda e: e.tensor_scalar(
                    TMP, pb(1, 128), RC_[:, 32 + tc:33 + tc], M12[:, 2:3],
                    op0=OP.mult, op1=OP.mult)))(t),
                    ("pb1", "rc", "m12", "tmp"), ("tmp",))
                vop((lambda tc: (lambda e: e.scalar_tensor_tensor(
                    OT, pb(0, 128), RC_[:, tc:tc + 1], TMP,
                    op0=OP.mult, op1=OP.subtract)))(t),
                    ("pb0", "rc", "tmp", "ot"), ("ot",))
                vop(lambda e: e.tensor_tensor(OSQ, OT, OT, op=OP.mult),
                    ("ot", "osq"), ("osq",))
                vop(lambda e: e.tensor_reduce(SSQ, OSQ, axis=AX.X, op=OP.add),
                    ("osq", "ssq"), ("ssq",))
                sop(lambda e: e.activation(RMSV, SSQ, AF.Ln, bias=EPSR,
                                           scale=1.0 / 128.0),
                    ("ssq", "epsr", "rmsv"), ("rmsv",))
                sop(lambda e: e.activation(RMSV, RMSV, AF.Exp, scale=-0.5),
                    ("rmsv",), ("rmsv",))
                vop(lambda e: e.tensor_scalar(TMP2, OT, RMSV[:, 0:1], 0.2,
                                              op0=OP.mult, op1=OP.mult),
                    ("ot", "rmsv", "tmp2"), ("tmp2",))
                nsl = slice(t * 128, (t + 1) * 128)
                split(NH[:, nsl], NL[:, nsl], TMP2, ("tmp2",), "nh", "nl")
            gamma_step(5)

        if u == 0:
            dump("dbg_nh", NH.bitcast(F32), ("nh",))
            dump("dbg_nl", NL.bitcast(F32), ("nl",))

        for aa in range(0 if stop_qc is not None else 2):
            tile = u * 2 + aa
            sch.fence()

            def wo_mms(jc):
                par = jc % 2
                for rc in range(16):
                    nsl = slice((aa * 16 + rc) * 128, (aa * 16 + rc + 1) * 128)
                    wsl = slice(rc * 256, (rc + 1) * 256)
                    mm(pb(4, 256), NH[:, nsl], WOH[par][:, wsl], rc == 0, False,
                       ("nh", f"woh{par}"), ("pb4",))
                    mm(pb(4, 256), NH[:, nsl], WOL[par][:, wsl], False, False,
                       ("nh", f"wol{par}"), ("pb4",))
                    mm(pb(4, 256), NL[:, nsl], WOH[par][:, wsl], False, rc == 15,
                       ("nl", f"woh{par}"), ("pb4",))
                src = bass.AP(xresg.tensor,
                              b * 128 * 4 * D + (2 * hl + aa) * D + jc * 256,
                              [[4 * D, 128], [1, 256]])
                dma("gpsimd", XRES, src, reads=("xres",), writes=("xres",))
                vop((lambda jcc: (lambda e: e.scalar_tensor_tensor(
                    YT[:, jcc * 256:(jcc + 1) * 256], pb(4, 256), 1.0, XRES,
                    op0=OP.mult, op1=OP.add)))(jc),
                    ("pb4", "xres", "yt"), ("yt",))

            for jc in range(8):
                par = jc % 2
                src = bass.AP(woT.tensor, jc * 256,
                              [[D, 128], [128 * D, 16], [1, 256]])
                dma("sync", WOTRAW, src, writes=("wotraw",))
                split(WOH[par], WOL[par], WOTRAW, ("wotraw",),
                      f"woh{par}", f"wol{par}")
                if jc > 0:
                    wo_mms(jc - 1)
            wo_mms(7)
            vop(lambda e: e.tensor_reduce(MEANV, YT, axis=AX.X, op=OP.add),
                ("yt", "meanv"), ("meanv",))
            vop(lambda e: e.tensor_scalar(MEANV, MEANV, 1.0 / D, None, op0=OP.mult),
                ("meanv",), ("meanv",))
            vop(lambda e: e.tensor_tensor(WOLNSQ, YT, YT, op=OP.mult),
                ("yt", "vh", "vl", "wolnsq"), ("wolnsq",))
            vop(lambda e: e.tensor_reduce(VARV, WOLNSQ, axis=AX.X, op=OP.add),
                ("wolnsq", "varv"), ("varv",))
            vop(lambda e: e.tensor_tensor(M2V, MEANV, MEANV, op=OP.mult),
                ("meanv", "m2v"), ("m2v",))
            vop(lambda e: e.tensor_scalar(VARV, VARV, 1.0 / D, None, op0=OP.mult),
                ("varv",), ("varv",))
            vop(lambda e: e.tensor_tensor(VARV, VARV, M2V, op=OP.subtract),
                ("varv", "m2v"), ("varv",))
            sop(lambda e: e.activation(RSTD, VARV, AF.Ln, bias=EPSL),
                ("varv", "epsl", "rstd"), ("rstd",))
            sop(lambda e: e.activation(RSTD, RSTD, AF.Exp, scale=-0.5),
                ("rstd",), ("rstd",))
            vop(lambda e: e.tensor_scalar(YT, YT, MEANV[:, 0:1], RSTD[:, 0:1],
                                          op0=OP.subtract, op1=OP.mult),
                ("yt", "meanv", "rstd"), ("yt",))
            if u == 0 and aa == 0:
                dump("dbg_yt", YT, ("yt",))
            for jc2 in range(16):
                mm(pb(5, 128), YT[:, jc2 * 128:(jc2 + 1) * 128], IDENT, True, True,
                   ("yt", "ident"), ("pb5",), tr=True)
                xsl = slice(jc2 * 1024 + tile * 128, jc2 * 1024 + (tile + 1) * 128)
                split(XLNH[:, xsl], XLNL[:, xsl], pb(5, 128), ("pb5",),
                      "xlnh", "xlnl")

    if not do_ffn:
        sch.emit(nc)
        return nc
    sch.fence()
    if dbg:
        dump("dbg_xlnh", XLNH.bitcast(F32), ("xlnh",))
        dump("dbg_xlnl", XLNL.bitcast(F32), ("xlnl",))
    gamma_step(len(GG) - gidx[0])
    for wi in range(3):
        vop((lambda w_: (lambda e: e.tensor_copy(GSM[:, 0:1],
                                                 GA3[:, w_:w_ + 1])))(wi),
            ("ga3", "gsm"), ("gsm",))
        vop((lambda w_: (lambda e: e.tensor_copy(GSM[:, 1:2],
                                                 GA3[:, w_:w_ + 1])))(wi),
            ("ga3", "gsm"), ("gsm",))
        mm(pb(3, 2), ONESC, GSM[:, 0:2], True, True, ("onesc", "gsm"), ("pb3",))
        vop(lambda e: e.tensor_scalar(GSM[0:1, 0:1], pb(3, 1, 0, 1),
                                      1.0 / float(HH * D), FFN_EPS,
                                      op0=OP.mult, op1=OP.add),
            ("pb3", "gsm"), ("gsm",))
        vop(lambda e: e.tensor_scalar(GSM[0:1, 1:2], GSM[0:1, 0:1], 0.5, None,
                                      op0=OP.mult), ("gsm",), ("gsm",))
        mm(pb(3, 2), ONESR, GSM[0:1, 0:2], True, True, ("onesr", "gsm"), ("pb3",))
        vop((lambda tv: (lambda e: e.tensor_copy(tv, pb(3, 2)[:, 1:2])))(THV[wi]),
            ("pb3",), (f"th{wi}",))

    for rc in range(NRC):
        sch.fence()

        def ffn_a_mms(hc):
            par = hc % 2
            for jc in range(NDC):
                xsl = slice(jc * 1024 + rc * RCW, jc * 1024 + (rc + 1) * RCW)
                wsl = slice(jc * 128, (jc + 1) * 128)
                mm(pb(0, RCW), Q1B[par][:, wsl], XLNH[:, xsl], jc == 0, False,
                   (f"q1b{par}", "xlnh"), ("pb0",))
                mm(pb(0, RCW), Q1B[par][:, wsl], XLNL[:, xsl], False, jc == NDC - 1,
                   (f"q1b{par}", "xlnl"), ("pb0",))
                mm(pb(1, RCW), Q2B[par][:, wsl], XLNH[:, xsl], jc == 0, False,
                   (f"q2b{par}", "xlnh"), ("pb1",))
                mm(pb(1, RCW), Q2B[par][:, wsl], XLNL[:, xsl], False, jc == NDC - 1,
                   (f"q2b{par}", "xlnl"), ("pb1",))
            sop(lambda e: e.activation(SG, pb(0, RCW), AF.Sigmoid),
                ("pb0", "sg"), ("sg",))
            vop(lambda e: e.tensor_tensor(HFT, SG, pb(1, RCW), op=OP.mult),
                ("sg", "pb1", "hft"), ("hft",))
            hsl = slice(hc * RCW, (hc + 1) * RCW)
            split(HIH[:, hsl], HIL[:, hsl], HFT, ("hft",), "hih", "hil")

        for hc in range(NHC):
            par = hc % 2
            src = bass.AP(w1b.tensor, hc * D * 128,
                          [[128, 128], [128 * 128, 16], [1, 128]])
            dma("sync", F1RAW, src, writes=("fw1",))
            vop(lambda e: e.tensor_scalar(T2B, F1RAW, -1.0, THV[0][:, 0:1],
                                          op0=OP.mult, op1=OP.is_gt),
                ("fw1", "th0", "t2b"), ("t2b",))
            vop((lambda p_: (lambda e: e.scalar_tensor_tensor(
                Q1B[p_], F1RAW, THV[0][:, 0:1], T2B,
                op0=OP.is_gt, op1=OP.subtract)))(par),
                ("fw1", "t2b", "th0", f"q1b{par}"), (f"q1b{par}",))
            src = bass.AP(w2b.tensor, hc * D * 128,
                          [[128, 128], [128 * 128, 16], [1, 128]])
            dma("gpsimd", F2RAW, src, writes=("fw2",))
            vop(lambda e: e.tensor_scalar(T2B2, F2RAW, -1.0, THV[1][:, 0:1],
                                          op0=OP.mult, op1=OP.is_gt),
                ("fw2", "th1", "t2b2"), ("t2b2",))
            vop((lambda p_: (lambda e: e.scalar_tensor_tensor(
                Q2B[p_], F2RAW, THV[1][:, 0:1], T2B2,
                op0=OP.is_gt, op1=OP.subtract)))(par),
                ("fw2", "t2b2", "th1", f"q2b{par}"), (f"q2b{par}",))
            if hc > 0:
                ffn_a_mms(hc - 1)
        ffn_a_mms(NHC - 1)
        if dbg and rc == 0:
            dump("dbg_hih", HIH.bitcast(F32), ("hih",))
            dump("dbg_hil", HIL.bitcast(F32), ("hil",))

        sch.fence()

        def ffn_b_half(g):
            dc, half, par = g // 2, g % 2, g % 2
            for hc in range(22):
                hcg = half * 22 + hc
                hsl = slice(hcg * RCW, (hcg + 1) * RCW)
                wsl = slice(hc * 128, (hc + 1) * 128)
                mm(pb(2, RCW), Q3H[par][:, wsl], HIH[:, hsl], hcg == 0, False,
                   (f"q3h{par}", "hih"), ("pb2",))
                mm(pb(2, RCW), Q3H[par][:, wsl], HIL[:, hsl], False,
                   hcg == NHC - 1, (f"q3h{par}", "hil"), ("pb2",))
            if half == 0:
                return
            xsl = slice(dc * 1024 + rc * RCW, dc * 1024 + (rc + 1) * RCW)
            vop((lambda x_: (lambda e: e.scalar_tensor_tensor(
                YF, pb(2, RCW), 1.0, x_, op0=OP.mult, op1=OP.add)))(XLNH[:, xsl]),
                ("pb2", "xlnh", "yf"), ("yf",))
            vop((lambda x_: (lambda e: e.tensor_tensor(YF, YF, x_, op=OP.add)))(
                XLNL[:, xsl]), ("yf", "xlnl"), ("yf",))
            for sub in range(2):
                mm(pb(3, 128), YF[:, sub * 128:(sub + 1) * 128], IDENT,
                   True, True, ("yf", "ident"), ("pb3",), tr=True)
                vop((lambda dcc, s_: (lambda e: e.tensor_copy(
                    FT[s_][:, dcc * 128:(dcc + 1) * 128], pb(3, 128))))(dc, sub),
                    ("pb3", "ft"), ("ft",))

        for g in range(NDC * 2):
            dc, half, par = g // 2, g % 2, g % 2
            src = bass.AP(w3b.tensor, (dc * HH + half * 22 * 128) * 128,
                          [[128, 128], [128 * 128, 22], [1, 128]])
            dma("sync" if par == 0 else "gpsimd", F3RAW, src, writes=("fw3",))
            vop(lambda e: e.tensor_scalar(T3B, F3RAW, -1.0, THV[2][:, 0:1],
                                          op0=OP.mult, op1=OP.is_gt),
                ("fw3", "th2", "t3b"), ("t3b",))
            vop((lambda p_: (lambda e: e.scalar_tensor_tensor(
                Q3H[p_], F3RAW, THV[2][:, 0:1], T3B,
                op0=OP.is_gt, op1=OP.subtract)))(par),
                ("fw3", "t3b", "th2", f"q3h{par}"), (f"q3h{par}",))
            if g > 0:
                ffn_b_half(g - 1)
        ffn_b_half(NDC * 2 - 1)

        for sub in range(2):
            tile = rc * 2 + sub
            F_ = FT[sub]
            vop((lambda Fa: (lambda e: e.tensor_reduce(MEANV, Fa, axis=AX.X,
                                                       op=OP.add)))(F_),
                ("ft", "meanv"), ("meanv",))
            vop(lambda e: e.tensor_scalar(MEANV, MEANV, 1.0 / D, None, op0=OP.mult),
                ("meanv",), ("meanv",))
            vop((lambda Fa: (lambda e: e.tensor_tensor(FLNSQ, Fa, Fa, op=OP.mult)))(F_),
                ("ft", "flnsq"), ("flnsq",))
            vop(lambda e: e.tensor_reduce(VARV, FLNSQ, axis=AX.X, op=OP.add),
                ("flnsq", "varv"), ("varv",))
            vop(lambda e: e.tensor_tensor(M2V, MEANV, MEANV, op=OP.mult),
                ("meanv", "m2v"), ("m2v",))
            vop(lambda e: e.tensor_scalar(VARV, VARV, 1.0 / D, None, op0=OP.mult),
                ("varv",), ("varv",))
            vop(lambda e: e.tensor_tensor(VARV, VARV, M2V, op=OP.subtract),
                ("varv", "m2v"), ("varv",))
            sop(lambda e: e.activation(RSTD, VARV, AF.Ln, bias=EPSL),
                ("varv", "epsl", "rstd"), ("rstd",))
            sop(lambda e: e.activation(RSTD, RSTD, AF.Exp, scale=-0.5),
                ("rstd",), ("rstd",))
            vop((lambda Fa: (lambda e: e.tensor_scalar(
                Fa, Fa, MEANV[:, 0:1], RSTD[:, 0:1],
                op0=OP.subtract, op1=OP.mult)))(F_),
                ("ft", "meanv", "rstd"), ("ft",))
            bb = tile >> 2
            tt_ = tile & 3
            dstap = bass.AP(out.tensor, bb * 128 * 4 * D + tt_ * D,
                            [[4 * D, 128], [1, D]])
            add("gpsimd", (lambda d_, Fa: (lambda e: e.dma_start(out=d_, in_=Fa)))(
                dstap, F_), reads=("ft",), writes=("outw",))
    if dbg:
        add("gpsimd", (lambda e: e.dma_start(out=dbg_t["dbg_th"],
            in_=a_(th_off, 3))), reads=("th0", "th1", "th2"), writes=("outw",))

    sch.emit(nc)
    return nc


def kernel(L0_attn_x, L0_attn_wq, L0_attn_wk, L0_attn_wv, L0_attn_wo,
           L0_attn_lambda_q1, L0_attn_lambda_q2, L0_attn_lambda_k1,
           L0_attn_lambda_k2, L0_ffn_w1, L0_ffn_w2, L0_ffn_w3):
    import ml_dtypes
    x = np.ascontiguousarray(np.asarray(L0_attn_x, np.float32))
    xT = np.ascontiguousarray(x.transpose(0, 2, 1))
    xresv = np.ascontiguousarray(x.reshape(B, 128, 8, 4, D).transpose(2, 0, 1, 3, 4))
    wq = np.asarray(L0_attn_wq, np.float32)
    wk = np.asarray(L0_attn_wk, np.float32)
    wv = np.asarray(L0_attn_wv, np.float32)
    woTm = np.ascontiguousarray(np.asarray(L0_attn_wo, np.float32).T)
    w1 = np.asarray(L0_ffn_w1, np.float32)
    w2 = np.asarray(L0_ffn_w2, np.float32)
    w3 = np.asarray(L0_ffn_w3, np.float32)
    w1bm = np.ascontiguousarray(w1.T.reshape(D, NHC, 128).transpose(1, 0, 2)).reshape(NHC * D, 128)
    w2bm = np.ascontiguousarray(w2.T.reshape(D, NHC, 128).transpose(1, 0, 2)).reshape(NHC * D, 128)
    w3bm = np.ascontiguousarray(w3.T.reshape(HH, NDC, 128).transpose(1, 0, 2)).reshape(NDC * HH, 128)
    ident = np.eye(128, dtype=np.float32)
    identb = np.eye(128, dtype=ml_dtypes.bfloat16)
    onespb = np.zeros((128, 128), ml_dtypes.bfloat16)
    onespb[:, 0] = 1.0
    onesc = np.zeros((128, 128), np.float32)
    onesc[:, 0] = 1.0
    halfsel = np.zeros((128, 128), np.float32)
    halfsel[0:64, 0] = 1.0
    halfsel[64:128, 1] = 1.0
    onesr = np.ones((1, 128), np.float32)
    lkpm = np.zeros((128, 132), np.float32)
    lkpm[:, 0] = np.asarray(L0_attn_lambda_k1, np.float32)
    lkpm[:, 1] = np.asarray(L0_attn_lambda_k2, np.float32)
    lqdm = np.zeros((128, 4), np.float32)
    lqdm[:, 0] = lqdm[:, 1] = np.asarray(L0_attn_lambda_q1, np.float32)
    lqdm[:, 2] = lqdm[:, 3] = np.asarray(L0_attn_lambda_q2, np.float32)

    in_maps = []
    for c in range(NCORE):
        in_maps.append(dict(
            xT=xT, xresg=xresv[c],
            wqT=np.ascontiguousarray(wq[256 * c:256 * c + 256].T),
            wkT=np.ascontiguousarray(wk[256 * c:256 * c + 256].T),
            wvT=np.ascontiguousarray(wv[256 * c:256 * c + 256].T),
            woT=woTm, w1b=w1bm, w2b=w2bm, w3b=w3bm,
            lkp=lkpm, lqd=lqdm, identx=ident, identbx=identb, onespbx=onespb,
            onescx=onesc, halfselx=halfsel, onesrx=onesr,
        ))
    nc = build_nc()
    res = run_bass_kernel_spmd(nc, in_maps, core_ids=list(range(NCORE)))
    O = np.empty((B, S, D), np.float32)
    Ov = O.reshape(B, 128, 8, 4, D)
    for c in range(NCORE):
        Ov[:, :, c, :, :] = np.asarray(res.results[c]["outp"]).reshape(B, 128, 4, D)
    return O


# version 08 (final)
# speedup vs baseline: 1.0168x; 1.0168x over previous
import numpy as np
import concourse.bass as bass
import concourse.mybir as mybir
from concourse.bass_utils import run_bass_kernel_spmd

F32 = mybir.dt.float32
BF16 = mybir.dt.bfloat16
AF = mybir.ActivationFunctionType
OP = mybir.AluOpType
AX = mybir.AxisListType

B = 2
S = 4096
D = 2048
HH = 5632
NCORE = 8
CSCALE = float(1.0 / np.sqrt(128.0))
LAM_INIT = 0.8
RMS_EPS = 1.1920928955078125e-07
LN_EPS = 1e-5
FFN_EPS = 1e-8
NHC = HH // 128
NDC = D // 128
NKT = S // 128
NQC = S // 512
NRC = 4
RCW = 256


class Sched:

    INC = {"tensor": 1, "vector": 1, "scalar": 1, "sync": 16, "gpsimd": 16}
    ENGS = ("sync", "scalar", "vector", "tensor", "gpsimd")

    def __init__(self):
        self.ops = []

    def add(self, eng, fn, reads=(), writes=()):
        self.ops.append(("op", eng, fn, tuple(reads), tuple(writes)))

    def fence(self):
        self.ops.append(("fence",))

    def emit(self, nc):
        DGRP = {"ident": "dconst", "identb": "dconst", "onespb": "dconst",
                "onesc": "dconst", "hsel": "dconst", "onesr": "dconst",
                "lkp": "dconst", "lqd": "dconst",
                "wraw": "dwraw", "xt40": "dxt40", "xt41": "dxt41",
                "wotraw": "dwot",
                "fw1": "dfw1", "fw2": "dfw2", "fw3": "dfw3",
                "xres": "dxres", "outw": "doutw", "yt": "dgw"}
        semnames = (list(self.INC) + sorted(set(DGRP.values()))
                    + sorted(set(v + "g" for v in DGRP.values())))
        sems = {s: nc.alloc_semaphore("sem_" + s) for s in semnames}
        count = {s: 0 for s in semnames}
        idx = {e: 0 for e in self.INC}
        last_drain = {e: -1 for e in self.INC}
        water = {e: {s: 0 for s in semnames} for e in self.INC}
        last_write = {}
        readers = {}
        streams = {e: [] for e in self.INC}
        rv_idx = {}
        psum_read = {"vector": 0, "scalar": 0}

        def wait(eng, sname, val):
            if val > water[eng][sname]:
                streams[eng].append(("wait", sems[sname], val))
                water[eng][sname] = val

        for item in self.ops:
            if item[0] == "fence":
                snap = dict(count)
                for eng in self.INC:
                    for sname in semnames:
                        if sname != eng:
                            wait(eng, sname, snap[sname])
                continue
            _, eng, fn, reads, writes = item
            is_dma = eng in ("sync", "gpsimd")
            if is_dma:
                assert writes, "DMA op needs a write key"
                grp = DGRP[writes[0]] + ("g" if eng == "gpsimd" else "")
            need = {}
            drain = False
            for bb in reads:
                lw = last_write.get(bb)
                if lw is None:
                    continue
                we, wsem, wval, wi = lw
                if we == eng and not is_dma:
                    if we in ("vector", "scalar") and wi > last_drain[eng]:
                        drain = True
                elif we != eng:
                    need[wsem] = max(need.get(wsem, 0), wval)
            for bb in writes:
                lw = last_write.get(bb)
                if lw is not None:
                    we, wsem, wval, wi = lw
                    if we != eng:
                        need[wsem] = max(need.get(wsem, 0), wval)
                    elif not is_dma and we in ("vector", "scalar")                             and wi > last_drain[eng]:
                        drain = True
                for re, (rsem, rval) in readers.get(bb, {}).items():
                    if re != eng:
                        need[rsem] = max(need.get(rsem, 0), rval)
                    elif not is_dma and re in ("vector", "scalar"):
                        rwi = rv_idx.get((bb, re), -1)
                        if rwi > last_drain[eng]:
                            drain = True
            reads_psum = (eng in ("vector", "scalar")
                          and any(bb.startswith("pb") for bb in reads))
            if reads_psum:
                other = "scalar" if eng == "vector" else "vector"
                if psum_read[other]:
                    need[other] = max(need.get(other, 0), psum_read[other])
            if drain:
                streams[eng].append(("drain",))
                last_drain[eng] = idx[eng] - 1
            for sname, val in need.items():
                wait(eng, sname, val)
            sname = grp if is_dma else eng
            streams[eng].append(("op", fn, sems[sname], self.INC[eng]))
            count[sname] += self.INC[eng]
            idx[eng] += 1
            if reads_psum:
                psum_read[eng] = count[sname]
            for bb in writes:
                last_write[bb] = (eng, sname, count[sname], idx[eng] - 1)
                readers[bb] = {}
                for kk in [k for k in rv_idx if k[0] == bb]:
                    del rv_idx[kk]
            for bb in reads:
                readers.setdefault(bb, {})[eng] = (sname, count[sname])
                rv_idx[(bb, eng)] = idx[eng] - 1

        streams["gpsimd"].append(("wait", sems["doutwg"], count["doutwg"]))

        with nc.Block() as block:
            for eng in self.ENGS:
                def make(eng_name):
                    def body(e):
                        for it in streams[eng_name]:
                            if it[0] == "drain":
                                e.drain()
                            elif it[0] == "wait":
                                e.wait_ge(it[1], it[2])
                            else:
                                it[1](e).then_inc(it[2], it[3])
                    return body
                getattr(block, eng)(make(eng))

def build_nc(dbg=False, stop_u=4, do_ffn=True, stop_qc=None, stop_kt=None,
             stop_sc=None, qkv_mode="full"):
    nc = bass.Bass("TRN2", target_bir_lowering=False)

    xT = nc.dram_tensor("xT", [B, D, S], F32, kind="ExternalInput").ap()
    xresg = nc.dram_tensor("xresg", [B, 128, 4, D], F32, kind="ExternalInput").ap()
    wqT = nc.dram_tensor("wqT", [D, 256], F32, kind="ExternalInput").ap()
    wkT = nc.dram_tensor("wkT", [D, 256], F32, kind="ExternalInput").ap()
    wvT = nc.dram_tensor("wvT", [D, 256], F32, kind="ExternalInput").ap()
    woT = nc.dram_tensor("woT", [D, D], F32, kind="ExternalInput").ap()
    w1b = nc.dram_tensor("w1b", [NHC * D, 128], F32, kind="ExternalInput").ap()
    w2b = nc.dram_tensor("w2b", [NHC * D, 128], F32, kind="ExternalInput").ap()
    w3b = nc.dram_tensor("w3b", [NDC * HH, 128], F32, kind="ExternalInput").ap()
    lkp = nc.dram_tensor("lkp", [128, 132], F32, kind="ExternalInput").ap()
    lqd = nc.dram_tensor("lqd", [128, 4], F32, kind="ExternalInput").ap()
    identx = nc.dram_tensor("identx", [128, 128], F32, kind="ExternalInput").ap()
    identbx = nc.dram_tensor("identbx", [128, 128], BF16, kind="ExternalInput").ap()
    onespbx = nc.dram_tensor("onespbx", [128, 128], BF16, kind="ExternalInput").ap()
    onescx = nc.dram_tensor("onescx", [128, 128], F32, kind="ExternalInput").ap()
    halfselx = nc.dram_tensor("halfselx", [128, 128], F32, kind="ExternalInput").ap()
    onesrx = nc.dram_tensor("onesrx", [1, 128], F32, kind="ExternalInput").ap()
    out = nc.dram_tensor("outp", [B, 128, 4, D], F32, kind="ExternalOutput").ap()

    dbg_t = {}
    if dbg:
        for nm, sh, dt_ in [("dbg_qh", [128, 2048], F32), ("dbg_ql", [128, 2048], F32),
                            ("dbg_kh", [128, 2048], F32), ("dbg_kl", [128, 2048], F32),
                            ("dbg_vh", [128, 2048], F32), ("dbg_vl", [128, 2048], F32),
                            ("dbg_nh", [128, 2048], F32), ("dbg_nl", [128, 2048], F32),
                            ("dbg_m12", [128, 4], F32), ("dbg_th", [128, 3], F32),
                            ("dbg_xlnh", [128, 8192], F32), ("dbg_xlnl", [128, 8192], F32),
                            ("dbg_hih", [128, 5632], F32), ("dbg_hil", [128, 5632], F32),
                            ("dbg_yt", [128, 2048], F32), ("dbg_u1", [128, 512], F32),
                            ("dbg_r1", [1, 512], F32)]:
            dbg_t[nm] = nc.dram_tensor(nm, sh, dt_, kind="ExternalOutput").ap()

    NCOL = 49120
    AR = nc.alloc_sbuf_tensor("arena", [128, NCOL], F32)
    ps = nc.alloc_psum_tensor("ps", [128, 8 * 512], F32)

    cur = [0]

    def alloc(w):
        off = cur[0]
        cur[0] += w
        assert cur[0] <= NCOL, f"arena overflow at {cur[0]}"
        return off

    def a_(off, w, p0=0, p1=128):
        return AR[p0:p1, off:off + w]

    def b_(off, nelem, p0=0, p1=128):
        assert nelem % 2 == 0
        return AR[p0:p1, off:off + nelem // 2].bitcast(BF16)

    def pb(i, w=512, p0=0, p1=128):
        return ps[p0:p1, i * 512:i * 512 + w]

    def pbbf(i, nelem, p0=0, p1=128):
        return ps[p0:p1, i * 512:i * 512 + nelem // 2].bitcast(BF16)

    IDENT = a_(alloc(128), 128)
    IDENTB = b_(alloc(64), 128)
    ONESPB = b_(alloc(64), 128)
    ONESC = a_(alloc(128), 128)
    HSEL = a_(alloc(128), 128)
    ONESR = a_(alloc(128), 128, 0, 1)
    lkp_off = alloc(132)
    LKP = a_(lkp_off, 132)
    LKW1 = a_(lkp_off, 128)
    LKW2 = a_(lkp_off + 1, 128)
    LQD = a_(alloc(4), 4)
    EPSR = a_(alloc(1), 1)
    EPSL = a_(alloc(1), 1)
    M12 = a_(alloc(4), 4)
    th_off = alloc(3)
    THV = [a_(th_off, 1), a_(th_off + 1, 1), a_(th_off + 2, 1)]
    GACC = a_(alloc(1), 1)
    GCOL = a_(alloc(1), 1)
    GSM = a_(alloc(2), 2)
    MROW = a_(alloc(4), 4, 0, 1)
    NX = a_(alloc(4), 4, 0, 2)
    NXC = a_(alloc(1), 1, 0, 2)
    LAMS = a_(alloc(1), 1, 0, 1)
    RT = a_(alloc(64), 64)
    RC_ = a_(alloc(64), 64)
    OT = a_(alloc(128), 128)
    OSQ = a_(alloc(128), 128)
    TMP = a_(alloc(128), 128)
    TMP2 = a_(alloc(128), 128)
    XLEV = TMP2
    SSQ = a_(alloc(1), 1)
    RMSV = a_(alloc(1), 1)
    MEANV = a_(alloc(1), 1)
    VARV = a_(alloc(1), 1)
    M2V = a_(alloc(1), 1)
    RSTD = a_(alloc(1), 1)
    SQS = a_(alloc(256), 256)
    WOEV = SQS[:, 0:256]
    H2EV = SQS[:, 0:256]
    Y2EV = SQS[:, 0:256]
    UE1T = SQS[:, 0:128]
    UE2T = SQS[:, 128:256]

    cur[0] = (cur[0] + 31) // 32 * 32
    CB = cur[0]
    cur[0] += 8192
    WRAW = a_(CB, 2048)
    WH = [b_(CB + 2048 + 1024 * (2 * i), 2048) for i in range(3)]
    WL = [b_(CB + 2048 + 1024 * (2 * i + 1), 2048) for i in range(3)]
    F1RAW = a_(CB, 2048)
    F2RAW = a_(CB + 2048, 2048)
    Q1B = [b_(CB + 4096 + 1024 * i, 2048) for i in range(2)]
    Q2B = [b_(CB + 6144 + 1024 * i, 2048) for i in range(2)]
    F3RAW = a_(CB, 2816)
    T3B = b_(CB + 2816, 2816)
    Q3H = [b_(CB + 4224 + 1408 * i, 2816) for i in range(2)]

    QB = cur[0]
    cur[0] += 12288
    QH = b_(QB, 4096)
    QL = b_(QB + 2048, 4096)
    KH = b_(QB + 4096, 4096)
    KL = b_(QB + 6144, 4096)
    VH = b_(QB + 8192, 4096)
    VL = b_(QB + 10240, 4096)
    WOH = [b_(QB + 2048 * i, 4096) for i in range(2)]
    WOL = [b_(QB + 4096 + 2048 * i, 4096) for i in range(2)]
    HIH = b_(QB, 11264)
    HIL = b_(QB + 5632, 11264)
    T2B = b_(QB + 11264, 2048)

    XLB_ = cur[0]
    cur[0] += 16384
    XLNH = b_(XLB_, 16384)
    XLNL = b_(XLB_ + 8192, 16384)

    SCRB = cur[0]
    cur[0] += 8192
    XT4 = [a_(SCRB + 512 * i, 512) for i in range(2)]
    XPH4 = [b_(SCRB + 1024 + 256 * i, 512) for i in range(2)]
    XPL4 = [b_(SCRB + 1536 + 256 * i, 512) for i in range(2)]
    SQS4 = a_(SCRB + 3072, 512)
    QEV = a_(SCRB + 3584, 512)
    KEV = a_(SCRB + 4096, 512)
    VEV = a_(SCRB + 4608, 512)
    NXEV = a_(SCRB + 5120, 512, 0, 2)
    E1R = a_(SCRB, 512)
    E2R = a_(SCRB + 512, 512)
    EH1 = [b_(SCRB + 1024 + 1024 * i, 512) for i in range(2)]
    EL1 = [b_(SCRB + 1280 + 1024 * i, 512) for i in range(2)]
    EH2 = [b_(SCRB + 1536 + 1024 * i, 512) for i in range(2)]
    EL2 = [b_(SCRB + 1792 + 1024 * i, 512) for i in range(2)]
    U1E = a_(SCRB + 3072, 512)
    U2E = a_(SCRB + 3584, 512)
    NH = b_(SCRB + 4096, 4096)
    NL = b_(SCRB + 6144, 4096)
    WOTRAW = a_(SCRB, 4096)
    FT = [a_(SCRB, 2048), a_(SCRB + 2048, 2048)]
    HFT = a_(SCRB + 4096, 256)
    FLNSQ = a_(SCRB + 4352, 2048)
    WOLNSQ = a_(QB + 8192, 2048)
    YF = a_(SCRB + 6400, 256)
    T2B2 = b_(SCRB + 6656, 2048)

    YT = a_(alloc(2048), 2048)
    SG = a_(alloc(256), 256)
    XRES = SG
    GA3 = a_(alloc(3), 3)

    sch = Sched()
    add = sch.add

    def dma(queue, dst, src, reads=(), writes=()):
        add(queue, (lambda d, s: (lambda e: e.dma_start(out=d, in_=s)))(dst, src),
            reads=reads, writes=writes)

    def mm(dst, lhsT, rhs, st, sp, reads, writes, tr=False):
        add("tensor",
            (lambda d, l, r, s1, s2, t: (lambda e: e.matmul(
                d, l, r, start=s1, stop=s2) if not t else e.matmul(
                d, l, r, is_transpose=True)))(dst, lhsT, rhs, st, sp, tr),
            reads=reads, writes=writes)

    def vop(fn, reads, writes):
        add("vector", fn, reads=reads, writes=writes)

    def sop(fn, reads, writes):
        add("scalar", fn, reads=reads, writes=writes)

    def split(hi, lo, src, rkeys, hikey, lokey, scale=None):
        if scale is None:
            vop((lambda h, s: (lambda e: e.tensor_copy(h, s)))(hi, src),
                tuple(rkeys) + (hikey,), (hikey,))
        else:
            vop((lambda h, s: (lambda e: e.tensor_scalar(
                h, s, scale, None, op0=OP.mult)))(hi, src),
                tuple(rkeys) + (hikey,), (hikey,))
        vop((lambda l, s, h, sc_: (lambda e: e.scalar_tensor_tensor(
            l, s, sc_, h, op0=OP.mult, op1=OP.subtract)))(
            lo, src, hi, 1.0 if scale is None else scale),
            tuple(rkeys) + (hikey, lokey), (lokey,))

    def dump(nm, src_ap, keys):
        if dbg:
            add("gpsimd", (lambda d, s: (lambda e: e.dma_start(out=d, in_=s)))(
                dbg_t[nm], src_ap), reads=keys, writes=("outw",))

    import os as _os
    _p0parts = 31 if _os.environ.get("SKIPP0", "0") != "1" else 0
    _p0parts = int(_os.environ.get("P0PARTS", str(_p0parts)))
    if _p0parts & 1:
        dma("sync", IDENT, identx, writes=("ident",))
        dma("sync", IDENTB, identbx, writes=("identb",))
        dma("sync", ONESPB, onespbx, writes=("onespb",))
        dma("sync", ONESC, onescx, writes=("onesc",))
        dma("sync", HSEL, halfselx, writes=("hsel",))
        dma("sync", ONESR, onesrx, writes=("onesr",))
        dma("sync", LKP, lkp, writes=("lkp",))
        dma("sync", LQD, lqd, writes=("lqd",))
    if _p0parts & 2:
        vop(lambda e: e.memset(EPSR, RMS_EPS), (), ("epsr",))
        vop(lambda e: e.memset(EPSL, LN_EPS), (), ("epsl",))
    if _p0parts & 4:
        mm(pb(6, 2), LKW1, LQD[:, 0:2], True, True, ("lkp", "lqd"), ("pb6",))
        mm(pb(7, 2), LKW2, LQD[:, 2:4], True, True, ("lkp", "lqd"), ("pb7",))
    if _p0parts & 8:
        sop(lambda e: e.activation(GSM[0:1, 0:1], pb(6, 1, 0, 1), AF.Exp), ("pb6",), ("gsm",))
        sop(lambda e: e.activation(GSM[0:1, 1:2], pb(7, 1, 0, 1), AF.Exp),
            ("pb6", "pb7", "gsm"), ("gsm",))
    if _p0parts & 16:
        vop(lambda e: e.tensor_tensor(LAMS, GSM[0:1, 0:1], GSM[0:1, 1:2], op=OP.subtract),
            ("gsm",), ("lams",))
        vop(lambda e: e.tensor_scalar(LAMS, LAMS, LAM_INIT, None, op0=OP.add),
            ("lams",), ("lams",))

    vop(lambda e: e.memset(GA3, 0.0), (), ("ga3",))
    GG = []
    for _wi, _wsrc in ((0, w1b), (1, w2b), (2, w3b)):
        _nt = (NHC * D if _wi < 2 else NDC * HH) // 128
        for _g0 in range(0, _nt, 16):
            GG.append((_wi, _wsrc, _g0))
    gidx = [0]

    def gamma_step(n):
        for _ in range(n):
            if gidx[0] >= len(GG):
                return
            wi, wsrc, g0 = GG[gidx[0]]
            gidx[0] += 1
            src = bass.AP(wsrc.tensor, g0 * 128 * 128,
                          [[128, 128], [128 * 128, 16], [1, 128]])
            dma("sync", YT, src, writes=("yt",))
            vop(lambda e: e.tensor_reduce(GCOL, YT, axis=AX.X, op=OP.add,
                                          apply_absolute_value=True),
                ("yt", "gcol"), ("gcol",))
            vop((lambda w_: (lambda e: e.tensor_tensor(
                GA3[:, w_:w_ + 1], GA3[:, w_:w_ + 1], GCOL, op=OP.add)))(wi),
                ("gcol", "ga3"), ("ga3",))

    for u in range(stop_u):
        b = u >> 1
        hl = u & 1
        sch.fence()

        for prj, wsrc in enumerate([wqT, wkT, wvT]):
            src = bass.AP(wsrc.tensor, hl * 128,
                          [[256, 128], [128 * 256, NDC], [1, 128]])
            dma("sync", WRAW, src, reads=(f"wh{prj}", f"wl{prj}"),
                writes=("wraw",))
            split(WH[prj], WL[prj], WRAW, ("wraw",), f"wh{prj}", f"wl{prj}")
        vop(lambda e: e.memset(NX[0:2, 0:2], 0.0), ("nx",), ("nx",))

        def qkv_step_load(sc4, dc):
            par = dc % 2
            src = bass.AP(xT.tensor, b * D * S + dc * 128 * S + sc4 * 512,
                          [[S, 128], [1, 512]])
            dma("sync", XT4[par], src, writes=(f"xt4{par}",))
            split(XPH4[par], XPL4[par], XT4[par], (f"xt4{par}",),
                  f"xph4{par}", f"xpl4{par}")

        def qkv_step_mms(sc4, dc):
            par = dc % 2
            for prj in range(3):
                whs = WH[prj][:, dc * 128:(dc + 1) * 128]
                wls = WL[prj][:, dc * 128:(dc + 1) * 128]
                mm(pb(prj, 512), whs, XPH4[par], dc == 0, False,
                   (f"wh{prj}", f"xph4{par}"), (f"pb{prj}",))
                mm(pb(prj, 512), whs, XPL4[par], False, False,
                   (f"wh{prj}", f"xpl4{par}"), (f"pb{prj}",))
                mm(pb(prj, 512), wls, XPH4[par], False, dc == NDC - 1,
                   (f"wl{prj}", f"xph4{par}"), (f"pb{prj}",))

        def qkv_tail(sc4):
            qsl = slice(sc4 * 512, (sc4 + 1) * 512)
            sop(lambda e: e.activation(QEV, pb(0, 512), AF.Copy),
                ("pb0", "qev"), ("qev",))
            sop(lambda e: e.activation(KEV, pb(1, 512), AF.Copy, scale=CSCALE),
                ("pb1", "kev"), ("kev",))
            sop(lambda e: e.activation(VEV, pb(2, 512), AF.Copy),
                ("pb2", "vev"), ("vev",))
            split(QH[:, qsl], QL[:, qsl], QEV, ("qev",), "qh", "ql")
            split(KH[:, qsl], KL[:, qsl], KEV, ("kev",), "kh", "kl")
            split(VH[:, qsl], VL[:, qsl], VEV, ("vev",), "vh", "vl")
            for col, pk in [(0, "pb0"), (1, "pb1")]:
                sc_ = 1.0 if col == 0 else CSCALE
                add("scalar", (lambda p_, s_: (lambda e: e.activation(
                    SQS4, p_, AF.Square, scale=s_)))(pb(col, 512), sc_),
                    reads=(pk, "sqs4"), writes=("sqs4",))
                mm(pb(3, 512), HSEL, SQS4, True, True, ("hsel", "sqs4"), ("pb3",))
                sop(lambda e: e.activation(NXEV, pb(3, 512, 0, 2), AF.Copy),
                    ("pb3", "nxev"), ("nxev",))
                vop(lambda e: e.tensor_reduce(NXC, NXEV, axis=AX.X, op=OP.max),
                    ("nxev", "nxc"), ("nxc",))
                vop((lambda cl: (lambda e: e.tensor_tensor(
                    NX[0:2, cl:cl + 1], NX[0:2, cl:cl + 1], NXC, op=OP.max)))(col),
                    ("nxc", "nx"), ("nx",))

        nsc4 = stop_sc if stop_sc is not None else S // 512
        steps = [(s4, d) for s4 in range(nsc4) for d in range(NDC)]
        for i, (sc4, dc) in enumerate(steps):
            qkv_step_load(sc4, dc)
            if i > 0:
                psc4, pdc = steps[i - 1]
                qkv_step_mms(psc4, pdc)
                if pdc == NDC - 1:
                    qkv_tail(psc4)
        if steps:
            qkv_step_mms(*steps[-1])
            qkv_tail(steps[-1][0])
        if stop_sc is not None:
            continue

        vop(lambda e: e.tensor_tensor(NX[0:2, 2:3], NX[0:2, 0:1], NX[0:2, 1:2],
                                      op=OP.mult), ("nx",), ("nx",))
        sop(lambda e: e.activation(NX[0:2, 2:3], NX[0:2, 2:3], AF.Ln,
                                   bias=EPSR[0:2, 0:1]), ("nx", "epsr"), ("nx",))
        sop(lambda e: e.activation(NX[0:2, 3:4], NX[0:2, 2:3], AF.Exp, scale=0.5),
            ("nx",), ("nx",))
        sop(lambda e: e.activation(NX[0:2, 3:4], NX[0:2, 3:4], AF.Copy, scale=-1.0),
            ("nx",), ("nx",))
        mm(ps[0:1, 3 * 512:3 * 512 + 2], NX[0:2, 3:4], IDENT[0:2, 0:2],
           True, True, ("nx", "ident"), ("pb3",), tr=True)
        sop(lambda e: e.activation(MROW[0:1, 0:2],
                                   ps[0:1, 3 * 512:3 * 512 + 2], AF.Copy),
            ("pb3", "mrow"), ("mrow",))
        vop(lambda e: e.tensor_copy(MROW[0:1, 2:3], LAMS), ("lams", "mrow"), ("mrow",))
        mm(pb(3, 3), ONESR, MROW[0:1, 0:3], True, True, ("onesr", "mrow"), ("pb3",))
        sop(lambda e: e.activation(M12[:, 0:3], pb(3, 3), AF.Copy),
            ("pb3", "m12"), ("m12",))

        if u == 0:
            dump("dbg_qh", QH.bitcast(F32), ("qh",))
            dump("dbg_ql", QL.bitcast(F32), ("ql",))
            dump("dbg_kh", KH.bitcast(F32), ("kh",))
            dump("dbg_kl", KL.bitcast(F32), ("kl",))
            dump("dbg_m12", M12, ("m12",))

        for t in range(NKT):
            tsl = slice(t * 128, (t + 1) * 128)
            mm(pbbf(3, 128 * 128 // 64)[:, 0:128] if False else
               ps[:, 3 * 512:3 * 512 + 64].bitcast(BF16), VH[:, tsl], IDENTB,
               True, True, ("vh", "identb"), ("pb3",), tr=True)
            sop((lambda d: (lambda e: e.activation(
                d, ps[:, 3 * 512:3 * 512 + 64].bitcast(BF16), AF.Copy)))(
                VH[:, tsl]), ("pb3", "vh"), ("vh",))
            mm(ps[:, 3 * 512:3 * 512 + 64].bitcast(BF16), VL[:, tsl], IDENTB,
               True, True, ("vl", "identb"), ("pb3",), tr=True)
            sop((lambda d: (lambda e: e.activation(
                d, ps[:, 3 * 512:3 * 512 + 64].bitcast(BF16), AF.Copy)))(
                VL[:, tsl]), ("pb3", "vl"), ("vl",))

        if u == 0:
            dump("dbg_vh", VH.bitcast(F32), ("vh",))
            dump("dbg_vl", VL.bitcast(F32), ("vl",))

        if stop_qc == 0:
            continue
        sch.fence()

        def scores_exp(qc, kt):
            bs = kt % 2
            S1 = pb(0) if bs == 0 else pb(1)
            S2 = pb(2) if bs == 0 else pb(3)
            k1 = "pb0" if bs == 0 else "pb1"
            k2 = "pb2" if bs == 0 else "pb3"
            qsl = slice(qc * 512, (qc + 1) * 512)
            ksl = slice(kt * 128, (kt + 1) * 128)
            mm(S1, KH[0:64, ksl], QH[0:64, qsl], True, False, ("kh", "qh"), (k1,))
            mm(S1, KH[0:64, ksl], QL[0:64, qsl], False, False, ("kh", "ql"), (k1,))
            mm(S1, KL[0:64, ksl], QH[0:64, qsl], False, True, ("kl", "qh"), (k1,))
            mm(S2, KH[64:128, ksl], QH[64:128, qsl], True, False, ("kh", "qh"), (k2,))
            mm(S2, KH[64:128, ksl], QL[64:128, qsl], False, False, ("kh", "ql"), (k2,))
            mm(S2, KL[64:128, ksl], QH[64:128, qsl], False, True, ("kl", "qh"), (k2,))
            sop((lambda S_: (lambda e: e.activation(E1R, S_, AF.Exp,
                                                    bias=M12[:, 0:1])))(S1),
                (k1, "m12", "e1r"), ("e1r",))
            sop((lambda S_: (lambda e: e.activation(E2R, S_, AF.Exp,
                                                    bias=M12[:, 1:2])))(S2),
                (k2, "m12", "e2r"), ("e2r",))
            vop((lambda d_: (lambda e: e.tensor_copy(d_, E1R)))(EH1[bs]),
                ("e1r", f"eh1{bs}"), (f"eh1{bs}",))
            vop((lambda d_: (lambda e: e.tensor_copy(d_, E2R)))(EH2[bs]),
                ("e2r", f"eh2{bs}"), (f"eh2{bs}",))
            vop((lambda l_, h_: (lambda e: e.scalar_tensor_tensor(
                l_, E1R, 1.0, h_, op0=OP.mult, op1=OP.subtract)))(
                EL1[bs], EH1[bs]), ("e1r", f"eh1{bs}", f"el1{bs}"),
                (f"el1{bs}",))
            vop((lambda l_, h_: (lambda e: e.scalar_tensor_tensor(
                l_, E2R, 1.0, h_, op0=OP.mult, op1=OP.subtract)))(
                EL2[bs], EH2[bs]), ("e2r", f"eh2{bs}", f"el2{bs}"),
                (f"el2{bs}",))

        def usums(kt):
            bs = kt % 2
            st = kt == 0
            sp = kt == NKT - 1
            vsl = slice(kt * 128, (kt + 1) * 128)
            mm(pb(4), VH[:, vsl], EH1[bs], st, False, ("vh", f"eh1{bs}"), ("pb4",))
            mm(pb(4), VH[:, vsl], EL1[bs], False, False, ("vh", f"el1{bs}"), ("pb4",))
            mm(pb(4), VL[:, vsl], EH1[bs], False, sp, ("vl", f"eh1{bs}"), ("pb4",))
            mm(pb(5), VH[:, vsl], EH2[bs], st, False, ("vh", f"eh2{bs}"), ("pb5",))
            mm(pb(5), VH[:, vsl], EL2[bs], False, False, ("vh", f"el2{bs}"), ("pb5",))
            mm(pb(5), VL[:, vsl], EH2[bs], False, sp, ("vl", f"eh2{bs}"), ("pb5",))
            mm(pb(6), ONESPB, EH1[bs], st, False, ("onespb", f"eh1{bs}"), ("pb6",))
            mm(pb(6), ONESPB, EL1[bs], False, sp, ("onespb", f"el1{bs}"), ("pb6",))
            mm(pb(7), ONESPB, EH2[bs], st, False, ("onespb", f"eh2{bs}"), ("pb7",))
            mm(pb(7), ONESPB, EL2[bs], False, sp, ("onespb", f"el2{bs}"), ("pb7",))

        for qc in range(stop_qc if stop_qc is not None else NQC):
            if stop_kt is not None:
                for kt in range(stop_kt):
                    scores_exp(qc, kt)
                break
            for kt in range(NKT):
                scores_exp(qc, kt)
                if kt > 0:
                    usums(kt - 1)
            usums(NKT - 1)
            sop(lambda e: e.activation(U1E, pb(4), AF.Copy),
                ("pb4", "u1e"), ("u1e",))
            sop(lambda e: e.activation(U2E, pb(5), AF.Copy),
                ("pb5", "u2e"), ("u2e",))
            sop(lambda e: e.activation(E1R[0:1, :], pb(6, 512, 0, 1), AF.Copy),
                ("pb6", "e1r"), ("r1e", "e1r"))
            sop(lambda e: e.activation(E2R[0:1, :], pb(7, 512, 0, 1), AF.Copy),
                ("pb7", "e2r"), ("r2e", "e2r"))
            if u == 0 and qc == 0:
                dump("dbg_u1", U1E, ("u1e",))
                dump("dbg_r1", E1R[0:1, :], ("r1e",))
            for t4 in range(4):
                t = qc * 4 + t4
                t4s = slice(t4 * 128, (t4 + 1) * 128)
                mm(pb(0, 128), U1E[:, t4s], IDENT, True, True,
                   ("u1e", "ident"), ("pb0",), tr=True)
                mm(pb(1, 128), U2E[:, t4s], IDENT, True, True,
                   ("u2e", "ident"), ("pb1",), tr=True)
                mm(pb(2, 1), E1R[0:1, t4s], IDENT[0:1, 0:1], True, True,
                   ("r1e", "e1r", "ident"), ("pb2",), tr=True)
                mm(pb(3, 1), E2R[0:1, t4s], IDENT[0:1, 0:1], True, True,
                   ("r2e", "e2r", "ident"), ("pb3",), tr=True)
                sop(lambda e: e.activation(UE1T, pb(0, 128), AF.Copy),
                    ("pb0", "ue1t"), ("ue1t",))
                sop(lambda e: e.activation(UE2T, pb(1, 128), AF.Copy),
                    ("pb1", "ue2t"), ("ue2t",))
                sop((lambda tc: (lambda e: e.activation(
                    RT[:, tc:tc + 1], pb(2, 1), AF.Copy)))(t),
                    ("pb2", "rt"), ("rt",))
                sop((lambda tc: (lambda e: e.activation(
                    RT[:, 32 + tc:33 + tc], pb(3, 1), AF.Copy)))(t),
                    ("pb3", "rt"), ("rt",))
                vop((lambda tc: (lambda e: e.reciprocal(RC_[:, tc:tc + 1],
                                                        RT[:, tc:tc + 1])))(t),
                    ("rt", "rc"), ("rc",))
                vop((lambda tc: (lambda e: e.reciprocal(RC_[:, 32 + tc:33 + tc],
                                                        RT[:, 32 + tc:33 + tc])))(t),
                    ("rt", "rc"), ("rc",))
                vop((lambda tc: (lambda e: e.tensor_scalar(
                    TMP, UE2T, RC_[:, 32 + tc:33 + tc], M12[:, 2:3],
                    op0=OP.mult, op1=OP.mult)))(t),
                    ("ue2t", "rc", "m12", "tmp"), ("tmp",))
                vop((lambda tc: (lambda e: e.scalar_tensor_tensor(
                    OT, UE1T, RC_[:, tc:tc + 1], TMP,
                    op0=OP.mult, op1=OP.subtract)))(t),
                    ("ue1t", "rc", "tmp", "ot"), ("ot",))
                vop(lambda e: e.tensor_tensor(OSQ, OT, OT, op=OP.mult),
                    ("ot", "osq"), ("osq",))
                vop(lambda e: e.tensor_reduce(SSQ, OSQ, axis=AX.X, op=OP.add),
                    ("osq", "ssq"), ("ssq",))
                sop(lambda e: e.activation(RMSV, SSQ, AF.Ln, bias=EPSR,
                                           scale=1.0 / 128.0),
                    ("ssq", "epsr", "rmsv"), ("rmsv",))
                sop(lambda e: e.activation(RMSV, RMSV, AF.Exp, scale=-0.5),
                    ("rmsv",), ("rmsv",))
                vop(lambda e: e.tensor_scalar(TMP2, OT, RMSV[:, 0:1], 0.2,
                                              op0=OP.mult, op1=OP.mult),
                    ("ot", "rmsv", "tmp2"), ("tmp2",))
                nsl = slice(t * 128, (t + 1) * 128)
                split(NH[:, nsl], NL[:, nsl], TMP2, ("tmp2",), "nh", "nl")
            gamma_step(5)

        if u == 0:
            dump("dbg_nh", NH.bitcast(F32), ("nh",))
            dump("dbg_nl", NL.bitcast(F32), ("nl",))

        for aa in range(0 if stop_qc is not None else 2):
            tile = u * 2 + aa
            sch.fence()

            def wo_mms(jc):
                par = jc % 2
                for rc in range(16):
                    nsl = slice((aa * 16 + rc) * 128, (aa * 16 + rc + 1) * 128)
                    wsl = slice(rc * 256, (rc + 1) * 256)
                    mm(pb(4, 256), NH[:, nsl], WOH[par][:, wsl], rc == 0, False,
                       ("nh", f"woh{par}"), ("pb4",))
                    mm(pb(4, 256), NH[:, nsl], WOL[par][:, wsl], False, False,
                       ("nh", f"wol{par}"), ("pb4",))
                    mm(pb(4, 256), NL[:, nsl], WOH[par][:, wsl], False, rc == 15,
                       ("nl", f"woh{par}"), ("pb4",))
                src = bass.AP(xresg.tensor,
                              b * 128 * 4 * D + (2 * hl + aa) * D + jc * 256,
                              [[4 * D, 128], [1, 256]])
                dma("gpsimd", XRES, src, reads=("xres",), writes=("xres",))
                sop(lambda e: e.activation(WOEV, pb(4, 256), AF.Copy),
                    ("pb4", "woev"), ("woev",))
                vop((lambda jcc: (lambda e: e.tensor_tensor(
                    YT[:, jcc * 256:(jcc + 1) * 256], WOEV, XRES, op=OP.add)))(jc),
                    ("woev", "xres", "yt"), ("yt",))

            for jc in range(8):
                par = jc % 2
                src = bass.AP(woT.tensor, jc * 256,
                              [[D, 128], [128 * D, 16], [1, 256]])
                dma("sync", WOTRAW, src, writes=("wotraw",))
                split(WOH[par], WOL[par], WOTRAW, ("wotraw",),
                      f"woh{par}", f"wol{par}")
                if jc > 0:
                    wo_mms(jc - 1)
            wo_mms(7)
            vop(lambda e: e.tensor_reduce(MEANV, YT, axis=AX.X, op=OP.add),
                ("yt", "meanv"), ("meanv",))
            vop(lambda e: e.tensor_scalar(MEANV, MEANV, 1.0 / D, None, op0=OP.mult),
                ("meanv",), ("meanv",))
            vop(lambda e: e.tensor_tensor(WOLNSQ, YT, YT, op=OP.mult),
                ("yt", "vh", "vl", "wolnsq"), ("wolnsq",))
            vop(lambda e: e.tensor_reduce(VARV, WOLNSQ, axis=AX.X, op=OP.add),
                ("wolnsq", "varv"), ("varv",))
            vop(lambda e: e.tensor_tensor(M2V, MEANV, MEANV, op=OP.mult),
                ("meanv", "m2v"), ("m2v",))
            vop(lambda e: e.tensor_scalar(VARV, VARV, 1.0 / D, None, op0=OP.mult),
                ("varv",), ("varv",))
            vop(lambda e: e.tensor_tensor(VARV, VARV, M2V, op=OP.subtract),
                ("varv", "m2v"), ("varv",))
            sop(lambda e: e.activation(RSTD, VARV, AF.Ln, bias=EPSL),
                ("varv", "epsl", "rstd"), ("rstd",))
            sop(lambda e: e.activation(RSTD, RSTD, AF.Exp, scale=-0.5),
                ("rstd",), ("rstd",))
            vop(lambda e: e.tensor_scalar(YT, YT, MEANV[:, 0:1], RSTD[:, 0:1],
                                          op0=OP.subtract, op1=OP.mult),
                ("yt", "meanv", "rstd"), ("yt",))
            if u == 0 and aa == 0:
                dump("dbg_yt", YT, ("yt",))
            for jc2 in range(16):
                mm(pb(5, 128), YT[:, jc2 * 128:(jc2 + 1) * 128], IDENT, True, True,
                   ("yt", "ident"), ("pb5",), tr=True)
                sop(lambda e: e.activation(XLEV, pb(5, 128), AF.Copy),
                    ("pb5", "xlev"), ("xlev",))
                xsl = slice(jc2 * 1024 + tile * 128, jc2 * 1024 + (tile + 1) * 128)
                split(XLNH[:, xsl], XLNL[:, xsl], XLEV, ("xlev",),
                      "xlnh", "xlnl")

    if not do_ffn:
        sch.emit(nc)
        return nc
    sch.fence()
    if dbg:
        dump("dbg_xlnh", XLNH.bitcast(F32), ("xlnh",))
        dump("dbg_xlnl", XLNL.bitcast(F32), ("xlnl",))
    gamma_step(len(GG) - gidx[0])
    for wi in range(3):
        vop((lambda w_: (lambda e: e.tensor_copy(GSM[:, 0:1],
                                                 GA3[:, w_:w_ + 1])))(wi),
            ("ga3", "gsm"), ("gsm",))
        vop((lambda w_: (lambda e: e.tensor_copy(GSM[:, 1:2],
                                                 GA3[:, w_:w_ + 1])))(wi),
            ("ga3", "gsm"), ("gsm",))
        mm(pb(3, 2), ONESC, GSM[:, 0:2], True, True, ("onesc", "gsm"), ("pb3",))
        sop(lambda e: e.activation(GSM[0:1, 0:1], pb(3, 1, 0, 1), AF.Copy),
            ("pb3", "gsm"), ("gsm",))
        vop(lambda e: e.tensor_scalar(GSM[0:1, 0:1], GSM[0:1, 0:1],
                                      1.0 / float(HH * D), FFN_EPS,
                                      op0=OP.mult, op1=OP.add),
            ("gsm",), ("gsm",))
        vop(lambda e: e.tensor_scalar(GSM[0:1, 1:2], GSM[0:1, 0:1], 0.5, None,
                                      op0=OP.mult), ("gsm",), ("gsm",))
        mm(pb(3, 2), ONESR, GSM[0:1, 0:2], True, True, ("onesr", "gsm"), ("pb3",))
        sop((lambda tv: (lambda e: e.activation(tv, pb(3, 2)[:, 1:2],
                                                AF.Copy)))(THV[wi]),
            ("pb3",), (f"th{wi}",))

    for rc in range(NRC):
        sch.fence()

        def ffn_a_mms(hc):
            par = hc % 2
            for jc in range(NDC):
                xsl = slice(jc * 1024 + rc * RCW, jc * 1024 + (rc + 1) * RCW)
                wsl = slice(jc * 128, (jc + 1) * 128)
                mm(pb(0, RCW), Q1B[par][:, wsl], XLNH[:, xsl], jc == 0, False,
                   (f"q1b{par}", "xlnh"), ("pb0",))
                mm(pb(0, RCW), Q1B[par][:, wsl], XLNL[:, xsl], False, jc == NDC - 1,
                   (f"q1b{par}", "xlnl"), ("pb0",))
                mm(pb(1, RCW), Q2B[par][:, wsl], XLNH[:, xsl], jc == 0, False,
                   (f"q2b{par}", "xlnh"), ("pb1",))
                mm(pb(1, RCW), Q2B[par][:, wsl], XLNL[:, xsl], False, jc == NDC - 1,
                   (f"q2b{par}", "xlnl"), ("pb1",))
            sop(lambda e: e.activation(SG, pb(0, RCW), AF.Sigmoid),
                ("pb0", "sg"), ("sg",))
            sop(lambda e: e.activation(H2EV, pb(1, RCW), AF.Copy),
                ("pb1", "h2ev"), ("h2ev",))
            vop(lambda e: e.tensor_tensor(HFT, SG, H2EV, op=OP.mult),
                ("sg", "h2ev", "hft"), ("hft",))
            hsl = slice(hc * RCW, (hc + 1) * RCW)
            split(HIH[:, hsl], HIL[:, hsl], HFT, ("hft",), "hih", "hil")

        for hc in range(NHC):
            par = hc % 2
            src = bass.AP(w1b.tensor, hc * D * 128,
                          [[128, 128], [128 * 128, 16], [1, 128]])
            dma("sync", F1RAW, src, writes=("fw1",))
            vop(lambda e: e.tensor_scalar(T2B, F1RAW, -1.0, THV[0][:, 0:1],
                                          op0=OP.mult, op1=OP.is_gt),
                ("fw1", "th0", "t2b"), ("t2b",))
            vop((lambda p_: (lambda e: e.scalar_tensor_tensor(
                Q1B[p_], F1RAW, THV[0][:, 0:1], T2B,
                op0=OP.is_gt, op1=OP.subtract)))(par),
                ("fw1", "t2b", "th0", f"q1b{par}"), (f"q1b{par}",))
            src = bass.AP(w2b.tensor, hc * D * 128,
                          [[128, 128], [128 * 128, 16], [1, 128]])
            dma("gpsimd", F2RAW, src, writes=("fw2",))
            vop(lambda e: e.tensor_scalar(T2B2, F2RAW, -1.0, THV[1][:, 0:1],
                                          op0=OP.mult, op1=OP.is_gt),
                ("fw2", "th1", "t2b2"), ("t2b2",))
            vop((lambda p_: (lambda e: e.scalar_tensor_tensor(
                Q2B[p_], F2RAW, THV[1][:, 0:1], T2B2,
                op0=OP.is_gt, op1=OP.subtract)))(par),
                ("fw2", "t2b2", "th1", f"q2b{par}"), (f"q2b{par}",))
            if hc > 0:
                ffn_a_mms(hc - 1)
        ffn_a_mms(NHC - 1)
        if dbg and rc == 0:
            dump("dbg_hih", HIH.bitcast(F32), ("hih",))
            dump("dbg_hil", HIL.bitcast(F32), ("hil",))

        sch.fence()

        def ffn_b_half(g):
            dc, half, par = g // 2, g % 2, g % 2
            for hc in range(22):
                hcg = half * 22 + hc
                hsl = slice(hcg * RCW, (hcg + 1) * RCW)
                wsl = slice(hc * 128, (hc + 1) * 128)
                mm(pb(2, RCW), Q3H[par][:, wsl], HIH[:, hsl], hcg == 0, False,
                   (f"q3h{par}", "hih"), ("pb2",))
                mm(pb(2, RCW), Q3H[par][:, wsl], HIL[:, hsl], False,
                   hcg == NHC - 1, (f"q3h{par}", "hil"), ("pb2",))
            if half == 0:
                return
            xsl = slice(dc * 1024 + rc * RCW, dc * 1024 + (rc + 1) * RCW)
            sop(lambda e: e.activation(Y2EV, pb(2, RCW), AF.Copy),
                ("pb2", "y2ev"), ("y2ev",))
            vop((lambda x_: (lambda e: e.scalar_tensor_tensor(
                YF, Y2EV, 1.0, x_, op0=OP.mult, op1=OP.add)))(XLNH[:, xsl]),
                ("y2ev", "xlnh", "yf"), ("yf",))
            vop((lambda x_: (lambda e: e.tensor_tensor(YF, YF, x_, op=OP.add)))(
                XLNL[:, xsl]), ("yf", "xlnl"), ("yf",))
            for sub in range(2):
                mm(pb(3, 128), YF[:, sub * 128:(sub + 1) * 128], IDENT,
                   True, True, ("yf", "ident"), ("pb3",), tr=True)
                sop((lambda dcc, s_: (lambda e: e.activation(
                    FT[s_][:, dcc * 128:(dcc + 1) * 128], pb(3, 128),
                    AF.Copy)))(dc, sub), ("pb3", "ft"), ("ft",))

        for g in range(NDC * 2):
            dc, half, par = g // 2, g % 2, g % 2
            src = bass.AP(w3b.tensor, (dc * HH + half * 22 * 128) * 128,
                          [[128, 128], [128 * 128, 22], [1, 128]])
            dma("sync" if par == 0 else "gpsimd", F3RAW, src, writes=("fw3",))
            vop(lambda e: e.tensor_scalar(T3B, F3RAW, -1.0, THV[2][:, 0:1],
                                          op0=OP.mult, op1=OP.is_gt),
                ("fw3", "th2", "t3b"), ("t3b",))
            vop((lambda p_: (lambda e: e.scalar_tensor_tensor(
                Q3H[p_], F3RAW, THV[2][:, 0:1], T3B,
                op0=OP.is_gt, op1=OP.subtract)))(par),
                ("fw3", "t3b", "th2", f"q3h{par}"), (f"q3h{par}",))
            if g > 0:
                ffn_b_half(g - 1)
        ffn_b_half(NDC * 2 - 1)

        for sub in range(2):
            tile = rc * 2 + sub
            F_ = FT[sub]
            vop((lambda Fa: (lambda e: e.tensor_reduce(MEANV, Fa, axis=AX.X,
                                                       op=OP.add)))(F_),
                ("ft", "meanv"), ("meanv",))
            vop(lambda e: e.tensor_scalar(MEANV, MEANV, 1.0 / D, None, op0=OP.mult),
                ("meanv",), ("meanv",))
            vop((lambda Fa: (lambda e: e.tensor_tensor(FLNSQ, Fa, Fa, op=OP.mult)))(F_),
                ("ft", "flnsq"), ("flnsq",))
            vop(lambda e: e.tensor_reduce(VARV, FLNSQ, axis=AX.X, op=OP.add),
                ("flnsq", "varv"), ("varv",))
            vop(lambda e: e.tensor_tensor(M2V, MEANV, MEANV, op=OP.mult),
                ("meanv", "m2v"), ("m2v",))
            vop(lambda e: e.tensor_scalar(VARV, VARV, 1.0 / D, None, op0=OP.mult),
                ("varv",), ("varv",))
            vop(lambda e: e.tensor_tensor(VARV, VARV, M2V, op=OP.subtract),
                ("varv", "m2v"), ("varv",))
            sop(lambda e: e.activation(RSTD, VARV, AF.Ln, bias=EPSL),
                ("varv", "epsl", "rstd"), ("rstd",))
            sop(lambda e: e.activation(RSTD, RSTD, AF.Exp, scale=-0.5),
                ("rstd",), ("rstd",))
            vop((lambda Fa: (lambda e: e.tensor_scalar(
                Fa, Fa, MEANV[:, 0:1], RSTD[:, 0:1],
                op0=OP.subtract, op1=OP.mult)))(F_),
                ("ft", "meanv", "rstd"), ("ft",))
            bb = tile >> 2
            tt_ = tile & 3
            dstap = bass.AP(out.tensor, bb * 128 * 4 * D + tt_ * D,
                            [[4 * D, 128], [1, D]])
            add("gpsimd", (lambda d_, Fa: (lambda e: e.dma_start(out=d_, in_=Fa)))(
                dstap, F_), reads=("ft",), writes=("outw",))
    if dbg:
        add("gpsimd", (lambda e: e.dma_start(out=dbg_t["dbg_th"],
            in_=a_(th_off, 3))), reads=("th0", "th1", "th2"), writes=("outw",))

    sch.emit(nc)
    return nc


def kernel(L0_attn_x, L0_attn_wq, L0_attn_wk, L0_attn_wv, L0_attn_wo,
           L0_attn_lambda_q1, L0_attn_lambda_q2, L0_attn_lambda_k1,
           L0_attn_lambda_k2, L0_ffn_w1, L0_ffn_w2, L0_ffn_w3):
    import ml_dtypes
    x = np.ascontiguousarray(np.asarray(L0_attn_x, np.float32))
    xT = np.ascontiguousarray(x.transpose(0, 2, 1))
    xresv = np.ascontiguousarray(x.reshape(B, 128, 8, 4, D).transpose(2, 0, 1, 3, 4))
    wq = np.asarray(L0_attn_wq, np.float32)
    wk = np.asarray(L0_attn_wk, np.float32)
    wv = np.asarray(L0_attn_wv, np.float32)
    woTm = np.ascontiguousarray(np.asarray(L0_attn_wo, np.float32).T)
    w1 = np.asarray(L0_ffn_w1, np.float32)
    w2 = np.asarray(L0_ffn_w2, np.float32)
    w3 = np.asarray(L0_ffn_w3, np.float32)
    w1bm = np.ascontiguousarray(w1.T.reshape(D, NHC, 128).transpose(1, 0, 2)).reshape(NHC * D, 128)
    w2bm = np.ascontiguousarray(w2.T.reshape(D, NHC, 128).transpose(1, 0, 2)).reshape(NHC * D, 128)
    w3bm = np.ascontiguousarray(w3.T.reshape(HH, NDC, 128).transpose(1, 0, 2)).reshape(NDC * HH, 128)
    ident = np.eye(128, dtype=np.float32)
    identb = np.eye(128, dtype=ml_dtypes.bfloat16)
    onespb = np.zeros((128, 128), ml_dtypes.bfloat16)
    onespb[:, 0] = 1.0
    onesc = np.zeros((128, 128), np.float32)
    onesc[:, 0] = 1.0
    halfsel = np.zeros((128, 128), np.float32)
    halfsel[0:64, 0] = 1.0
    halfsel[64:128, 1] = 1.0
    onesr = np.ones((1, 128), np.float32)
    lkpm = np.zeros((128, 132), np.float32)
    lkpm[:, 0] = np.asarray(L0_attn_lambda_k1, np.float32)
    lkpm[:, 1] = np.asarray(L0_attn_lambda_k2, np.float32)
    lqdm = np.zeros((128, 4), np.float32)
    lqdm[:, 0] = lqdm[:, 1] = np.asarray(L0_attn_lambda_q1, np.float32)
    lqdm[:, 2] = lqdm[:, 3] = np.asarray(L0_attn_lambda_q2, np.float32)

    in_maps = []
    for c in range(NCORE):
        in_maps.append(dict(
            xT=xT, xresg=xresv[c],
            wqT=np.ascontiguousarray(wq[256 * c:256 * c + 256].T),
            wkT=np.ascontiguousarray(wk[256 * c:256 * c + 256].T),
            wvT=np.ascontiguousarray(wv[256 * c:256 * c + 256].T),
            woT=woTm, w1b=w1bm, w2b=w2bm, w3b=w3bm,
            lkp=lkpm, lqd=lqdm, identx=ident, identbx=identb, onespbx=onespb,
            onescx=onesc, halfselx=halfsel, onesrx=onesr,
        ))
    nc = build_nc()
    res = run_bass_kernel_spmd(nc, in_maps, core_ids=list(range(NCORE)))
    O = np.empty((B, S, D), np.float32)
    Ov = O.reshape(B, 128, 8, 4, D)
    for c in range(NCORE):
        Ov[:, :, c, :, :] = np.asarray(res.results[c]["outp"]).reshape(B, 128, 4, D)
    return O
